# version 23
# speedup vs baseline: 1.0655x; 1.0046x over previous
import contextlib
import numpy as np
import ml_dtypes
import concourse.bass as bass
import concourse.mybir as mybir

F32 = mybir.dt.float32
BF16 = mybir.dt.bfloat16
AF = mybir.ActivationFunctionType
ALU = mybir.AluOpType
AX = mybir.AxisListType

D = 1024
B = 2
T = 8192
NCHUNK = 8
CH = 1024
HALF = 512
NT = 2048
NEG = -1.0e30


def core_chunks(c):
    b, p = c // 4, c % 4
    return b, p, 7 - p

def qtok_slice(c):
    b, ca, cb = core_chunks(c)
    return b, ca, cb

def unit_schedule():
    sched = []
    for j in range(NCHUNK):
        for half in range(2):
            if j <= 3:
                sched.append((0, j, half))
            sched.append((1, j, half))
    return sched

UNITS = unit_schedule()
NUNITS = len(UNITS)

def host_phase_a_inputs(c, x):
    b, ca, cb = core_chunks(c)
    p = c % 4
    qx = np.concatenate([x[b, ca*CH:(ca+1)*CH], x[b, cb*CH:(cb+1)*CH]], axis=0)
    kv = np.concatenate(
        [x[b, j*CH:(j+1)*CH] if j <= 7 - p else x[b, 0:CH] for j in range(NCHUNK)],
        axis=0)
    maskf = np.full((128, NUNITS * 8 * HALF), NEG, dtype=np.float32)
    part = np.arange(128, dtype=np.float32)
    kvc = np.arange(HALF, dtype=np.float32)
    for u, (qc, j, half) in enumerate(UNITS):
        cq = ca if qc == 0 else cb
        real = (j <= p) if qc == 0 else (j <= 7 - p)
        if not real:
            continue
        for qt in range(8):
            col = (u * 8 + qt) * HALF
            qg = cq * CH + qt * 128 + part
            kvg = j * CH + half * HALF + kvc
            maskf[:, col:col+HALF] = np.where(kvg[None, :] <= qg[:, None],
                                              0.0, NEG).astype(np.float32)
    identb = np.eye(128, dtype=ml_dtypes.bfloat16)
    return {"qx": np.ascontiguousarray(qx), "kvx": np.ascontiguousarray(kv),
            "maskf": maskf, "identb": identb}

def host_scatter_tokens(c, full_out, core_out):
    b, ca, cb = core_chunks(c)
    full_out[b, ca*CH:(ca+1)*CH] = core_out[0:CH]
    full_out[b, cb*CH:(cb+1)*CH] = core_out[CH:2*CH]


class Walk:
    def __init__(self, eng_name, E, sems):
        self.eng = eng_name
        self.E = E
        self.sems = sems
        self.cnt = {k: 0 for k in sems}
        self.notes = {}

    def on(self, eng):
        return eng == self.eng

    def wait(self, eng, sem, val=None):
        v = self.cnt[sem] if val is None else val
        if v <= 0:
            return
        if self.on(eng):
            self.E.wait_ge(self.sems[sem], v)

    def inc(self, eng, ins, sem, k=1):
        if self.on(eng) and ins is not None:
            ins.then_inc(self.sems[sem], k)
        self.cnt[sem] += k



PHASE_A_SEMS = ["wld0", "wld1", "wtp", "wcp", "cst", "xld0", "xld1",
                "xld2", "xld3",
                "sqd", "lnd", "xtd", "xcd",
                "tpj", "vpe", "sqe", "kst",
                "t1s", "vmx", "mng",
                "kld0", "kld1", "mld",
                "t2s", "vst2", "sxp", "ttr", "spt", "tav", "voa",
                "vfin", "st0", "st1", "vself", "sself"]

def phase_a_bufs(nc, ctx):
    ent = ctx.enter_context
    bufs = dict(
        wkT=ent(nc.sbuf_tensor("a_wkT", [128, 8192], BF16)),
        wqT=ent(nc.sbuf_tensor("a_wqT", [128, 8192], BF16)),
        wvT=ent(nc.sbuf_tensor("a_wvT", [128, 8192], BF16)),
        q=ent(nc.sbuf_tensor("a_q", [128, 16384], BF16)),
        O=ent(nc.sbuf_tensor("a_O", [128, 16384], F32)),
        k=[ent(nc.sbuf_tensor(f"a_k{i}", [128, 4096], BF16)) for i in range(2)],
        v=[ent(nc.sbuf_tensor(f"a_v{i}", [128, 4096], BF16)) for i in range(2)],
        xnfm=ent(nc.sbuf_tensor("a_xnfm", [128, 4096], BF16)),
        xt=[ent(nc.sbuf_tensor(f"a_xt{i}", [128, 1024], F32)) for i in range(2)],
        sqs=ent(nc.sbuf_tensor("a_sqs", [128, 1024], F32)),
        wtmp=[ent(nc.sbuf_tensor(f"a_wtmp{i}", [128, 1024], BF16)) for i in range(2)],
        identb=ent(nc.sbuf_tensor("a_identb", [128, 128], BF16)),
        m=ent(nc.sbuf_tensor("a_m", [128, 16], F32)),
        negm=ent(nc.sbuf_tensor("a_negm", [128, 16], F32)),
        rcol=ent(nc.sbuf_tensor("a_rcol", [128, 4], F32)),
        mcol=ent(nc.sbuf_tensor("a_mcol", [128, 256], F32)),
        lcol=ent(nc.sbuf_tensor("a_lcol", [128, 256], F32)),
        lncol=ent(nc.sbuf_tensor("a_lncol", [128, 32], F32)),
        epscol=ent(nc.sbuf_tensor("a_epscol", [128, 1], F32)),
        ps_sc=[ent(nc.psum_tensor(f"a_pssc{i}", [128, 512], F32)) for i in range(2)],
        ps_av=[ent(nc.psum_tensor(f"a_psav{i}", [128, 1024], F32)) for i in range(2)],
        ps_pt=ent(nc.psum_tensor("a_pspt", [128, 512], BF16)),
        ps_xnt=ent(nc.psum_tensor("a_psxnt", [128, 1024], BF16)),
    )
    a_wq = nc.lookup_mloc(bufs["wqT"]).addr
    a_wk = nc.lookup_mloc(bufs["wkT"]).addr
    a_wv = nc.lookup_mloc(bufs["wvT"]).addr
    bufs["maskw"] = nc.alloc_sbuf_tensor_at("a_maskw", [128, 4096], F32, offset=a_wq)
    bufs["p"] = [nc.alloc_sbuf_tensor_at(f"a_p{i}", [128, 512], BF16,
                                         offset=a_wk + i * 1024) for i in range(4)]
    bufs["pt"] = [nc.alloc_sbuf_tensor_at(f"a_pt{i}", [128, 512], BF16,
                                          offset=a_wk + 4096 + i * 1024) for i in range(2)]
    bufs["oout"] = [nc.alloc_sbuf_tensor_at(f"a_oout{i}", [128, 1024], F32,
                                            offset=a_wv + i * 4096) for i in range(2)]
    a_O = nc.lookup_mloc(bufs["O"]).addr
    bufs["xt"] = bufs["xt"] + [
        nc.alloc_sbuf_tensor_at(f"a_xt{i}", [128, 1024], F32, offset=a_O + (i-2)*4096)
        for i in range(2, 4)]
    return bufs


def phase_a_plan(w: Walk, bufs, io):
    E = w.E
    b = bufs
    ts = bass.ts

    ins = E.dma_start(out=b["identb"][:], in_=io["identb"]) if w.on("sync") else None
    w.inc("sync", ins, "cst", 16)

    if w.on("vector"):
        E.memset(b["mcol"][:], NEG)
        E.memset(b["lcol"][:], 0.0)
        ins = E.memset(b["epscol"][:], 1e-5)
    else:
        ins = None
    w.inc("vector", ins, "lnd", 1)

    wmats = [("wq", b["wqT"]), ("wk", b["wkT"]), ("wv", b["wvT"])]
    for s in range(24):
        mat, wT = wmats[s // 8]
        mc = s % 8
        par = s % 2
        if s >= 2:
            w.wait("gpsimd", "wtp", 8 * (s - 1))
        ins = E.dma_start(out=b["wtmp"][par][:],
                          in_=io[mat][mc*128:(mc+1)*128, :]) if w.on("gpsimd") else None
        w.inc("gpsimd", ins, "wld%d" % par, 16)
        if s == 0:
            w.wait("tensor", "cst", 16)
        w.wait("tensor", "wld%d" % par, 16 * (s // 2 + 1))
        if s >= 1:
            w.wait("tensor", "wcp", s)
        for kc in range(8):
            ins = E.transpose(b["ps_xnt"][:, ts(kc, 128)],
                              b["wtmp"][par][:, ts(kc, 128)],
                              b["identb"][:]) if w.on("tensor") else None
            w.inc("tensor", ins, "wtp", 1)
        w.wait("vector", "wtp", 8 * (s + 1))
        if w.on("vector"):
            dst = wT.ap().rearrange("p (k f) -> p k f", k=8)[:, :, mc*128:(mc+1)*128]
            src = b["ps_xnt"].ap().rearrange("p (k f) -> p k f", k=8)
            ins = E.tensor_copy(dst, src)
        else:
            ins = None
        w.inc("vector", ins, "wcp", 1)

    def ln_front(i, src_ap):
        par = i % 4
        cs = 8 * par
        if i >= 4:
            w.wait("sync", "sqd", 3 * (i - 4) + 1)
            w.wait("sync", "lnd", (i - 4) + 2)
        ins = E.dma_start(out=b["xt"][par][:], in_=src_ap) if w.on("sync") else None
        w.inc("sync", ins, "xld%d" % par, 16)
        w.wait("scalar", "xld%d" % par, w.cnt["xld%d" % par])
        ssq = b["lncol"][:, cs+1:cs+2]
        ins = E.activation(b["sqs"][:], b["xt"][par][:], AF.Square,
                           accum_out=ssq) if w.on("scalar") else None
        w.inc("scalar", ins, "sqd", 1)
        w.wait("vector", "xld%d" % par, w.cnt["xld%d" % par])
        ssum = b["lncol"][:, cs:cs+1]
        mean = b["lncol"][:, cs+2:cs+3]
        var = b["lncol"][:, cs+3:cs+4]
        m2 = b["lncol"][:, cs+4:cs+5]
        ins = E.reduce_sum(ssum, b["xt"][par][:], axis=AX.X) if w.on("vector") else None
        w.inc("vector", ins, "xtd", 1)
        w.wait("scalar", "xtd", w.cnt["xtd"])
        if i == 0:
            w.wait("scalar", "lnd", 1)
        ins = E.activation(mean, ssum, AF.Copy, scale=1.0 / D) if w.on("scalar") else None
        w.inc("scalar", ins, "sself", 1)
        w.wait("scalar", "sself", w.cnt["sself"])
        ins = E.activation(m2, mean, AF.Square) if w.on("scalar") else None
        w.inc("scalar", ins, "sqd", 1)
        w.wait("vector", "sqd", w.cnt["sqd"])
        ins = E.scalar_tensor_tensor(var, ssq, 1.0 / D, m2, ALU.mult,
                                     ALU.subtract) if w.on("vector") else None
        w.inc("vector", ins, "xtd", 1)
        w.wait("scalar", "xtd", w.cnt["xtd"])
        lnv = b["lncol"][:, cs+5:cs+6]
        rstd = b["lncol"][:, cs+6:cs+7]
        ins = E.activation(lnv, var, AF.Ln,
                           bias=b["epscol"][:, 0:1]) if w.on("scalar") else None
        w.inc("scalar", ins, "sself", 1)
        w.wait("scalar", "sself", w.cnt["sself"])
        ins = E.activation(rstd, lnv, AF.Exp, scale=-0.5) if w.on("scalar") else None
        w.inc("scalar", ins, "sqd", 1)
        w.notes[("rstd", i)] = w.cnt["sqd"]
        w.notes[("var", i)] = w.cnt["xtd"]

    def ln_back(i):
        par = i % 4
        tt = i % 4
        cs = 8 * par
        mean = b["lncol"][:, cs+2:cs+3]
        rstd = b["lncol"][:, cs+6:cs+7]
        w.wait("vector", "sqd", w.notes[("rstd", i)])
        w.wait("vector", "xtd", w.notes[("var", i)])
        prev = w.notes.get(("lnT", i - 2))
        if prev is not None:
            w.wait("vector", "xtd", prev)
        xnbf = b["wtmp"][i % 2]
        if w.on("vector"):
            ins = E.tensor_scalar(xnbf[:], b["xt"][par][:], mean, rstd,
                                  ALU.subtract, ALU.mult)
        else:
            ins = None
        w.inc("vector", ins, "lnd", 1)
        w.wait("tensor", "lnd", w.cnt["lnd"])
        if i == 0:
            w.wait("tensor", "wcp", 24)
        if i >= 1:
            w.wait("tensor", "xcd", i)
        for cc in range(8):
            ins = E.transpose(b["ps_xnt"][:, ts(cc, 128)], xnbf[:, ts(cc, 128)],
                              b["identb"][:]) if w.on("tensor") else None
            w.inc("tensor", ins, "xtd", 1)
        w.notes[("lnT", i)] = w.cnt["xtd"]
        w.wait("vector", "xtd", w.cnt["xtd"])
        tp = w.notes.get("tpj_half_done")
        if tp:
            w.wait("vector", "tpj", tp)
        if w.on("vector"):
            dst = b["xnfm"].ap().rearrange("p (k f) -> p k f", k=8)[:, :, tt*128:(tt+1)*128]
            src = b["ps_xnt"].ap().rearrange("p (k f) -> p k f", k=8)
            ins = E.tensor_copy(dst, src)
        else:
            ins = None
        w.inc("vector", ins, "xcd", 1)

    def ln_tile(i, src_ap):
        ln_front(i, src_ap)
        ln_back(i)

    def proj_half(wT, evac, n_groups=8, vproj=False):
        for gi in range(n_groups):
            par = w.cnt["tpj"] % 2
            ps = b["ps_av"][par].ap()[:, 0:512]
            if gi == 0:
                w.wait("tensor", "xcd", w.cnt["xcd"])
            if w.cnt["tpj"] >= 2:
                w.wait("tensor", "vpe", w.cnt["vpe"] - 1)
            for kc in range(8):
                if vproj:
                    tt, ch = gi // 2, gi % 2
                    lhsT = b["xnfm"][:, kc*512 + tt*128: kc*512 + (tt+1)*128]
                    rhs = wT[:, kc*1024 + ch*512: kc*1024 + (ch+1)*512]
                else:
                    lhsT = wT[:, kc*1024 + gi*128: kc*1024 + (gi+1)*128]
                    rhs = b["xnfm"][:, kc*512:(kc+1)*512]
                ins = E.matmul(ps, lhsT, rhs, start=(kc == 0),
                               stop=(kc == 7)) if w.on("tensor") else None
            w.inc("tensor", ins, "tpj", 1)
            evac(gi, ps, par)
        w.notes["tpj_half_done"] = w.cnt["tpj"]

    for qh in range(4):
        for t4 in range(4):
            i = qh * 4 + t4
            ln_tile(i, io["qx"][(qh*4+t4)*128:(qh*4+t4+1)*128, :])
        def q_evac(gi, ps, par, qh=qh):
            w.wait("scalar", "tpj", w.cnt["tpj"])
            dst = b["q"][:, gi*2048 + qh*512: gi*2048 + (qh+1)*512]
            ins = E.activation(dst, ps, AF.Copy,
                               scale=1.0/32.0) if w.on("scalar") else None
            w.inc("scalar", ins, "sqe", 1)
            w.wait("vector", "sqe", w.cnt["sqe"])
            ins = E.nop() if w.on("vector") else None
            w.inc("vector", ins, "vpe", 1)
        proj_half(b["wqT"], q_evac)

    def hc_units(j):
        return [0, 1] if j <= 3 else [1]

    kglob = 0
    uabs = 0

    def p1_units(j, hc, par):
        nonlocal kglob, uabs
        for qc in hc_units(j):
            for qt in range(8):
                k_ = kglob
                qq = qc * 8 + qt
                spar = k_ % 2
                w.wait("tensor", "vpe", w.notes[("vpe_hc", hc)])
                w.wait("tensor", "sqe", 32)
                if k_ >= 2:
                    w.wait("tensor", "vmx", k_ - 1)
                for cc in range(8):
                    lhsT = b["q"][:, cc*2048 + qc*1024 + qt*128:
                                  cc*2048 + qc*1024 + (qt+1)*128]
                    rhs = b["k"][par][:, cc*512:(cc+1)*512]
                    ins = E.matmul(b["ps_sc"][spar][:], lhsT, rhs,
                                   start=(cc == 0), stop=(cc == 7)) \
                        if w.on("tensor") else None
                w.inc("tensor", ins, "t1s", 1)
                w.wait("vector", "t1s", w.cnt["t1s"])
                ins = E.reduce_max(b["mcol"][:, qq*16 + hc: qq*16 + hc + 1],
                                   b["ps_sc"][spar][:],
                                   axis=AX.X) if w.on("vector") else None
                w.inc("vector", ins, "vmx", 1)
                kglob += 1
            uabs += 1
        w.notes[("t1s", hc)] = w.cnt["t1s"]
        w.notes[("kend", hc)] = kglob

    pend = None
    for j in range(NCHUNK):
        for half in range(2):
            hc = j * 2 + half
            par = hc % 2
            for t4 in range(4):
                i = 16 + hc * 4 + t4
                row = j * CH + half * HALF + t4 * 128
                ln_front(i, io["kvx"][row:row+128, :])
            if pend is not None:
                p1_units(*pend)
            for t4 in range(4):
                ln_back(16 + hc * 4 + t4)
            t1s_prev = w.notes.get(("t1s", hc - 2), 0)
            kst_prev = w.notes.get(("kst", hc - 2), 0)
            def k_evac(gi, ps, par_, t1s_prev=t1s_prev, kst_prev=kst_prev, hc=hc, par=par):
                w.wait("vector", "t1s", t1s_prev)
                w.wait("vector", "kst", kst_prev)
                w.wait("vector", "tpj", w.cnt["tpj"])
                ins = E.tensor_copy(b["k"][par][:, gi*512:(gi+1)*512],
                                    ps) if w.on("vector") else None
                w.inc("vector", ins, "vpe", 1)
            proj_half(b["wkT"], k_evac)
            def v_evac(gi, ps, par_, t1s_prev=t1s_prev, kst_prev=kst_prev, hc=hc, par=par):
                tt, ch = gi // 2, gi % 2
                w.wait("vector", "kst", kst_prev)
                w.wait("vector", "tpj", w.cnt["tpj"])
                ins = E.tensor_copy(b["v"][par][:, tt*1024 + ch*512: tt*1024 + (ch+1)*512],
                                    ps) if w.on("vector") else None
                w.inc("vector", ins, "vpe", 1)
            proj_half(b["wvT"], v_evac, vproj=True)
            w.notes[("vpe_hc", hc)] = w.cnt["vpe"]
            w.wait("gpsimd", "vpe", w.cnt["vpe"])
            for src, dst in ((b["k"][par], io["kf"]), (b["v"][par], io["vf"])):
                for hhf in range(2):
                    ins = E.dma_start(out=dst[:, hc*4096 + hhf*2048: hc*4096 + (hhf+1)*2048],
                                      in_=src[:, hhf*2048:(hhf+1)*2048]) \
                        if w.on("gpsimd") else None
                    w.inc("gpsimd", ins, "kst", 16)
            w.notes[("kst", hc)] = w.cnt["kst"]
            pend = (j, hc, par)
    p1_units(*pend)

    w.wait("vector", "vmx", w.cnt["vmx"])
    ins = E.memset(b["O"][:], 0.0) if w.on("vector") else None
    w.inc("vector", ins, "vself", 1)
    for qq in range(16):
        ins = E.reduce_max(b["m"][:, qq:qq+1], b["mcol"][:, qq*16:(qq+1)*16],
                           axis=AX.X) if w.on("vector") else None
        w.inc("vector", ins, "vself", 1)
    w.wait("vector", "vself", w.cnt["vself"])
    ins = E.tensor_scalar_mul(b["negm"][:], b["m"][:], -1.0) if w.on("vector") else None
    w.inc("vector", ins, "mng", 1)

    NK2 = kglob
    units2 = []
    k2 = 0
    ua2 = 0
    for j in range(NCHUNK):
        for half in range(2):
            hc = j * 2 + half
            for qc in hc_units(j):
                for qt in range(8):
                    units2.append((k2, qc, qt, ua2, hc))
                    k2 += 1
                ua2 += 1

    ps_ptp = [b["ps_pt"], b["ps_xnt"].ap()[:, 0:512]]

    def p2_transp(m):
        w.wait("tensor", "sxp", m + 1)
        if m >= 2:
            w.wait("tensor", "spt", m - 1)
        for tt in range(4):
            ins = E.transpose(ps_ptp[m % 2][:, ts(tt, 128)],
                              b["p"][m % 4][:, ts(tt, 128)],
                              b["identb"][:]) if w.on("tensor") else None
            w.inc("tensor", ins, "ttr", 1)

    def p2_ptcopy(m):
        w.wait("scalar", "ttr", 4 * (m + 1))
        if m >= 2:
            w.wait("scalar", "tav", m - 1)
        ins = E.copy(b["pt"][m % 2][:], ps_ptp[m % 2][:]) if w.on("scalar") else None
        w.inc("scalar", ins, "spt", 1)

    def p2_av(m):
        kk, qc, qt, ua, hc = units2[m]
        w.wait("tensor", "spt", m + 1)
        if m >= 2:
            w.wait("tensor", "voa", m - 1)
        for tt in range(4):
            for chn in range(2):
                ins = E.matmul(b["ps_av"][m % 2][:, chn*512:(chn+1)*512],
                               b["pt"][m % 2][:, ts(tt, 128)],
                               b["v"][hc % 2][:, tt*1024 + chn*512:
                                              tt*1024 + (chn+1)*512],
                               start=(tt == 0), stop=(tt == 3)) \
                    if w.on("tensor") else None
        w.inc("tensor", ins, "tav", 1)

    def p2_oadd(m):
        kk, qc, qt, ua, hc = units2[m]
        qq = qc * 8 + qt
        w.wait("vector", "tav", m + 1)
        ins = E.tensor_tensor(b["O"][:, qq*1024:(qq+1)*1024],
                              b["O"][:, qq*1024:(qq+1)*1024],
                              b["ps_av"][m % 2][:],
                              ALU.add) if w.on("vector") else None
        w.inc("vector", ins, "voa", 1)

    hc_loaded = -1
    for (k_, qc, qt, ua, hc) in units2:
        spar = k_ % 2
        qq = qc * 8 + qt
        if hc != hc_loaded:
            hc_loaded = hc
            par = hc % 2
            if hc < 2:
                w.wait("sync", "kst", w.cnt["kst"])
                w.wait("sync", "t1s", w.cnt["t1s"])
            else:
                kend = w.notes[("kend", hc - 2)]
                w.wait("sync", "t2s", kend)
                w.wait("sync", "tav", kend)
            for dst, src in ((b["k"][par], io["kf"]), (b["v"][par], io["vf"])):
                for hhf in range(2):
                    ins = E.dma_start(out=dst[:, hhf*2048:(hhf+1)*2048],
                                      in_=src[:, hc*4096 + hhf*2048: hc*4096 + (hhf+1)*2048]) \
                        if w.on("sync") else None
                    w.inc("sync", ins, "kld%d" % par, 16)
        if k_ == 0:
            w.wait("gpsimd", "tpj", 32)
        if k_ >= 8:
            w.wait("gpsimd", "vst2", k_ - 7)
        mc0 = (ua * 8 + qt) * 512
        ins = E.dma_start(out=b["maskw"][:, (k_ % 8)*512:(k_ % 8 + 1)*512],
                          in_=io["maskf"][:, mc0:mc0+512]) if w.on("gpsimd") else None
        w.inc("gpsimd", ins, "mld", 16)
        w.wait("tensor", "kld%d" % (hc % 2), w.cnt["kld%d" % (hc % 2)])
        if k_ == 0:
            w.wait("tensor", "mng", 1)
        if k_ >= 2:
            w.wait("tensor", "sxp", k_ - 1)
        for cc in range(8):
            lhsT = b["q"][:, cc*2048 + qc*1024 + qt*128:
                          cc*2048 + qc*1024 + (qt+1)*128]
            rhs = b["k"][hc % 2][:, cc*512:(cc+1)*512]
            ins = E.matmul(b["ps_sc"][spar][:], lhsT, rhs,
                           start=(cc == 0), stop=(cc == 7)) \
                if w.on("tensor") else None
        w.inc("tensor", ins, "t2s", 1)
        if k_ >= 1:
            p2_transp(k_ - 1)
            p2_ptcopy(k_ - 1)
        if k_ >= 2:
            p2_av(k_ - 2)
        w.wait("vector", "t2s", w.cnt["t2s"])
        w.wait("vector", "mld", 16 * (k_ + 1))
        ins = E.tensor_tensor(b["ps_sc"][spar][:], b["ps_sc"][spar][:],
                              b["maskw"][:, (k_ % 8)*512:(k_ % 8 + 1)*512],
                              ALU.add) if w.on("vector") else None
        w.inc("vector", ins, "vst2", 1)
        w.wait("scalar", "vst2", w.cnt["vst2"])
        if k_ == 0:
            w.wait("scalar", "mng", 1)
        if k_ >= 4:
            w.wait("scalar", "ttr", 4 * (k_ - 3))
        lslot = qq * 16 + hc
        ins = E.activation(b["p"][k_ % 4][:], b["ps_sc"][spar][:], AF.Exp,
                           bias=b["negm"][:, qq:qq+1],
                           accum_out=b["lcol"][:, lslot:lslot+1]) \
            if w.on("scalar") else None
        w.inc("scalar", ins, "sxp", 1)
        if k_ >= 2:
            p2_oadd(k_ - 2)
    p2_transp(NK2 - 1)
    p2_ptcopy(NK2 - 1)
    p2_av(NK2 - 2)
    p2_oadd(NK2 - 2)
    p2_av(NK2 - 1)
    p2_oadd(NK2 - 1)

    w.wait("vector", "voa", NK2)
    w.wait("vector", "sxp", NK2)
    for qq in range(16):
        par = qq % 2
        ins = E.reduce_sum(b["rcol"][:, par:par+1], b["lcol"][:, qq*16:(qq+1)*16],
                           axis=AX.X) if w.on("vector") else None
        w.inc("vector", ins, "vself", 1)
        w.wait("vector", "vself", w.cnt["vself"])
        ins = E.reciprocal(b["rcol"][:, 2+par:3+par],
                           b["rcol"][:, par:par+1]) if w.on("vector") else None
        w.inc("vector", ins, "vself", 1)
        if qq >= 2:
            w.wait("vector", "st%d" % par, 16 * (qq // 2))
        w.wait("vector", "vself", w.cnt["vself"])
        ins = E.tensor_scalar_mul(b["oout"][par][:], b["O"][:, qq*1024:(qq+1)*1024],
                                  b["rcol"][:, 2+par:3+par]) if w.on("vector") else None
        w.inc("vector", ins, "vfin", 1)
        w.wait("sync", "vfin", w.cnt["vfin"])
        ins = E.dma_start(out=io["s0"][qq*128:(qq+1)*128, :],
                          in_=b["oout"][par][:]) if w.on("sync") else None
        w.inc("sync", ins, "st%d" % par, 16)
    if w.on("sync"):
        E.wait_ge(w.sems["st0"], w.cnt["st0"])
        E.wait_ge(w.sems["st1"], w.cnt["st1"])


def build_phase_a_nc():
    import contextlib
    nc = bass.Bass("TRN2", target_bir_lowering=False, debug=False)
    io = {
        "qx": nc.dram_tensor("qx", [NT, D], F32, kind="ExternalInput").ap(),
        "kvx": nc.dram_tensor("kvx", [T, D], F32, kind="ExternalInput").ap(),
        "wq": nc.dram_tensor("wq", [D, D], F32, kind="ExternalInput").ap(),
        "wk": nc.dram_tensor("wk", [D, D], F32, kind="ExternalInput").ap(),
        "wv": nc.dram_tensor("wv", [D, D], F32, kind="ExternalInput").ap(),
        "maskf": nc.dram_tensor("maskf", [128, NUNITS * 8 * HALF], F32,
                                kind="ExternalInput").ap(),
        "identb": nc.dram_tensor("identb", [128, 128], BF16, kind="ExternalInput").ap(),
        "kf": nc.dram_tensor("kf", [128, 65536], BF16, kind="Internal").ap(),
        "vf": nc.dram_tensor("vf", [128, 65536], BF16, kind="Internal").ap(),
        "s0": nc.dram_tensor("s0", [NT, D], F32, kind="ExternalOutput").ap(),
    }
    with contextlib.ExitStack() as ctx:
        bufs = phase_a_bufs(nc, ctx)
        sems = {s: ctx.enter_context(nc.semaphore("sA_" + s)) for s in PHASE_A_SEMS}
        with nc.Block() as blk:
            @blk.sync
            def _(E):
                phase_a_plan(Walk("sync", E, sems), bufs, io)
            @blk.gpsimd
            def _(E):
                phase_a_plan(Walk("gpsimd", E, sems), bufs, io)
            @blk.tensor
            def _(E):
                phase_a_plan(Walk("tensor", E, sems), bufs, io)
            @blk.scalar
            def _(E):
                phase_a_plan(Walk("scalar", E, sems), bufs, io)
            @blk.vector
            def _(E):
                phase_a_plan(Walk("vector", E, sems), bufs, io)
    return nc



PHASE_B_SEMS = ["cst", "rprep", "xld0", "xld1", "sqd", "lnd", "xtd", "srt",
                "xcd", "x32d", "trt", "vlg",
                "wld0", "wld1", "wdl", "tgu", "ssil", "vh", "tdn", "vmx",
                "sfin", "vfin", "qld0", "qld1", "sld0", "sld1",
                "st0", "st1", "vself", "sself"]

def phase_b_bufs(nc, ctx):
    ent = ctx.enter_context
    bufs = dict(
        xnfm=ent(nc.sbuf_tensor("b_xnfm", [128, 8192], BF16)),
        xf32t=[ent(nc.sbuf_tensor(f"b_xf32t{i}", [128, 1024], F32)) for i in range(2)],
        h=ent(nc.sbuf_tensor("b_h", [128, 16384], BF16)),
        mixed=ent(nc.sbuf_tensor("b_mixed", [128, 8192], F32)),
        wg=[ent(nc.sbuf_tensor(f"b_wg{i}", [128, 4096], BF16)) for i in range(2)],
        wu=[ent(nc.sbuf_tensor(f"b_wu{i}", [128, 4096], BF16)) for i in range(2)],
        wd=ent(nc.sbuf_tensor("b_wd", [128, 16384], BF16)),
        xt=[ent(nc.sbuf_tensor(f"b_xt{i}", [128, 1024], F32)) for i in range(2)],
        xn32=[ent(nc.sbuf_tensor(f"b_xn32{i}", [128, 1024], F32)) for i in range(2)],
        sqs=ent(nc.sbuf_tensor("b_sqs", [128, 1024], F32)),
        tmpg=ent(nc.sbuf_tensor("b_tmpg", [128, 1024], BF16)),
        gbuf=[ent(nc.sbuf_tensor(f"b_gbuf{i}", [128, 1024], F32)) for i in range(2)],
        qt2=[ent(nc.sbuf_tensor(f"b_qt2{i}", [128, 1024], F32)) for i in range(2)],
        st2=[ent(nc.sbuf_tensor(f"b_st2{i}", [128, 1024], F32)) for i in range(2)],
        comb=ent(nc.sbuf_tensor("b_comb", [128, 64], F32)),
        lg=ent(nc.sbuf_tensor("b_lg", [128, 7], F32)),
        l2=ent(nc.sbuf_tensor("b_l2", [128, 7], F32)),
        eq1=ent(nc.sbuf_tensor("b_eq1", [128, 7], F32)),
        eq2=ent(nc.sbuf_tensor("b_eq2", [128, 7], F32)),
        rw=ent(nc.sbuf_tensor("b_rw", [7, 1024], F32)),
        rwT=ent(nc.sbuf_tensor("b_rwT", [128, 56], F32)),
        lncol=ent(nc.sbuf_tensor("b_lncol", [128, 16], F32)),
        rcol=ent(nc.sbuf_tensor("b_rcol", [128, 16], F32)),
        epscol=ent(nc.sbuf_tensor("b_epscol", [128, 1], F32)),
        identb=ent(nc.sbuf_tensor("b_identb", [128, 128], BF16)),
        identf=ent(nc.sbuf_tensor("b_identf", [128, 128], F32)),
        ps_g=[ent(nc.psum_tensor(f"b_psg{i}", [128, 1024], F32)) for i in range(2)],
        ps_u=[ent(nc.psum_tensor(f"b_psu{i}", [128, 1024], F32)) for i in range(2)],
    )
    bufs["ps_d"] = bufs["ps_u"][0]
    bufs["ps_sm"] = bufs["ps_u"][1]
    return bufs


def phase_b_plan(w: Walk, bufs, io):
    E = w.E
    b = bufs
    ts = bass.ts
    HC = 32
    NEXP = 8

    for nm in ("identb", "identf"):
        ins = E.dma_start(out=b[nm][:], in_=io[nm]) if w.on("sync") else None
        w.inc("sync", ins, "cst", 16)
    ins = E.dma_start(out=b["rw"][:], in_=io["router_w"]) if w.on("sync") else None
    w.inc("sync", ins, "cst", 16)
    if w.on("vector"):
        E.memset(b["epscol"][:], 1e-5)
        ins = E.memset(b["comb"][:], 1.0)
    else:
        ins = None
    w.inc("vector", ins, "lnd", 1)
    w.wait("tensor", "cst", 48)
    for kc in range(8):
        ins = E.transpose(b["ps_d"].ap()[:, kc*7:(kc+1)*7], b["rw"][0:7, ts(kc, 128)],
                          b["identf"][0:7, 0:7]) if w.on("tensor") else None
        w.inc("tensor", ins, "trt", 1)
    w.wait("vector", "trt", w.cnt["trt"])
    ins = E.tensor_copy(b["rwT"][:], b["ps_d"].ap()[:, 0:56]) if w.on("vector") else None
    w.inc("vector", ins, "rprep", 1)

    tile_i = 0

    def ln_tile_b(g, i):
        nonlocal tile_i
        idx = tile_i; tile_i += 1
        par = idx % 2
        cs = 8 * par
        if idx >= 2:
            w.wait("sync", "sqd", 2 * (idx - 2) + 1)
            w.wait("sync", "lnd", (idx - 2) + 2)
        ins = E.dma_start(out=b["xt"][par][:],
                          in_=io["qx"][(g*8+i)*128:(g*8+i+1)*128, :]) if w.on("sync") else None
        w.inc("sync", ins, "xld%d" % par, 16)
        w.wait("scalar", "xld%d" % par, w.cnt["xld%d" % par])
        ssq = b["lncol"][:, cs+1:cs+2]
        ins = E.activation(b["sqs"][:], b["xt"][par][:], AF.Square,
                           accum_out=ssq) if w.on("scalar") else None
        w.inc("scalar", ins, "sqd", 1)
        w.wait("vector", "xld%d" % par, w.cnt["xld%d" % par])
        ssum = b["lncol"][:, cs:cs+1]
        mean = b["lncol"][:, cs+2:cs+3]
        var = b["lncol"][:, cs+3:cs+4]
        m2 = b["lncol"][:, cs+4:cs+5]
        ins = E.reduce_sum(ssum, b["xt"][par][:], axis=AX.X) if w.on("vector") else None
        w.inc("vector", ins, "vself", 1)
        w.wait("vector", "sqd", w.cnt["sqd"])
        w.wait("vector", "vself", w.cnt["vself"])
        ins = E.tensor_scalar_mul(mean, ssum, 1.0 / D) if w.on("vector") else None
        w.inc("vector", ins, "vself", 1)
        w.wait("vector", "vself", w.cnt["vself"])
        ins = E.tensor_tensor(m2, mean, mean, ALU.mult) if w.on("vector") else None
        w.inc("vector", ins, "vself", 1)
        w.wait("vector", "vself", w.cnt["vself"])
        ins = E.scalar_tensor_tensor(var, ssq, 1.0 / D, m2, ALU.mult,
                                     ALU.subtract) if w.on("vector") else None
        w.inc("vector", ins, "xtd", 1)
        w.wait("scalar", "xtd", w.cnt["xtd"])
        lnv = b["lncol"][:, cs+5:cs+6]
        rstd = b["lncol"][:, cs+6:cs+7]
        if idx == 0:
            w.wait("scalar", "lnd", 1)
        ins = E.activation(lnv, var, AF.Ln,
                           bias=b["epscol"][:, 0:1]) if w.on("scalar") else None
        w.inc("scalar", ins, "sself", 1)
        w.wait("scalar", "sself", w.cnt["sself"])
        ins = E.activation(rstd, lnv, AF.Exp, scale=-0.5) if w.on("scalar") else None
        w.inc("scalar", ins, "sqd", 1)
        w.wait("vector", "sqd", w.cnt["sqd"])
        w.wait("vector", "xtd", w.cnt["xtd"])
        prev = w.notes.get(("lnT", idx - 2))
        if prev is not None:
            w.wait("vector", "xtd", prev)
        ins = E.tensor_scalar(b["xn32"][par][:], b["xt"][par][:], mean, rstd,
                              ALU.subtract, ALU.mult) if w.on("vector") else None
        w.inc("vector", ins, "lnd", 1)
        w.wait("tensor", "lnd", w.cnt["lnd"])
        w.wait("tensor", "vmx", w.cnt["vmx"])
        w.wait("tensor", "vh", w.cnt["vh"])
        w.wait("tensor", "ssil", w.cnt["ssil"])
        if idx >= 1:
            w.wait("tensor", "xcd", w.notes.get(("xc", idx - 1), 0))
        for cc in range(8):
            ins = E.transpose(b["ps_d"].ap()[:, cc*128:(cc+1)*128],
                              b["xn32"][par][:, ts(cc, 128)],
                              b["identf"][:]) if w.on("tensor") else None
            w.inc("tensor", ins, "xtd", 1)
        w.notes[("lnT", idx)] = w.cnt["xtd"]
        w.wait("vector", "xtd", w.cnt["xtd"])
        prevrt = w.notes.get(("rt", idx - 2))
        if prevrt is not None:
            w.wait("vector", "trt", prevrt)
        if w.on("vector"):
            dst = b["xnfm"].ap().rearrange("p (k f) -> p k f", k=8)[:, :, i*128:(i+1)*128]
            src = b["ps_d"].ap()[:, 0:1024].rearrange("p (k f) -> p k f", k=8)
            E.tensor_copy(dst, src)
            ins = E.tensor_copy(b["xf32t"][par][:], b["ps_d"].ap()[:, 0:1024])
        else:
            ins = None
        w.inc("vector", ins, "xcd", 2)
        w.notes[("xc", idx)] = w.cnt["xcd"]
        return idx

    def router_tile(g, i, idx):
        w.wait("tensor", "xcd", w.cnt["xcd"])
        w.wait("tensor", "rprep", 1)
        if w.cnt["vlg"] >= 1:
            w.wait("tensor", "vlg", w.cnt["vlg"] - 1)
        for kc in range(8):
            ins = E.matmul(b["ps_sm"].ap()[:, 0:7],
                           b["xf32t"][idx % 2][:, kc*128:(kc+1)*128],
                           b["rwT"][:, kc*7:(kc+1)*7],
                           start=(kc == 0), stop=(kc == 7)) if w.on("tensor") else None
        w.inc("tensor", ins, "trt", 1)
        w.notes[("rt", idx)] = w.cnt["trt"]
        w.wait("vector", "trt", w.cnt["trt"])
        def vop(fn, sem="vself"):
            ins = fn(E) if w.on("vector") else None
            w.inc("vector", ins, sem, 1)
            w.wait("vector", sem, w.cnt[sem])
        cb = b["comb"][:, i*8:i*8+7]
        v1 = b["rcol"][:, 0:1]; v2 = b["rcol"][:, 1:2]
        dd = b["rcol"][:, 2:3]; ee = b["rcol"][:, 3:4]
        t1 = b["rcol"][:, 4:5]; w1 = b["rcol"][:, 5:6]; w2 = b["rcol"][:, 6:7]
        ins = E.tensor_copy(b["lg"][:], b["ps_sm"].ap()[:, 0:7]) if w.on("vector") else None
        w.inc("vector", ins, "vlg", 1)
        w.wait("vector", "vlg", w.cnt["vlg"])
        vop(lambda E: E.reduce_max(v1, b["lg"][:], axis=AX.X))
        vop(lambda E: E.tensor_scalar(b["eq1"][:], b["lg"][:], v1, None, ALU.is_equal))
        vop(lambda E: E.scalar_tensor_tensor(b["l2"][:], b["eq1"][:], NEG, b["lg"][:],
                                             ALU.mult, ALU.add))
        vop(lambda E: E.reduce_max(v2, b["l2"][:], axis=AX.X))
        vop(lambda E: E.tensor_sub(dd, v2, v1))
        w.wait("scalar", "vself", w.cnt["vself"])
        ins = E.activation(ee, dd, AF.Exp) if w.on("scalar") else None
        w.inc("scalar", ins, "srt", 1)
        w.wait("vector", "srt", w.cnt["srt"])
        vop(lambda E: E.tensor_scalar_add(t1, ee, 1.0))
        vop(lambda E: E.reciprocal(w1, t1))
        vop(lambda E: E.tensor_scalar(w2, w1, -1.0, 1.0, ALU.mult, ALU.add))
        vop(lambda E: E.tensor_scalar(b["eq2"][:], b["l2"][:], v2, None, ALU.is_equal))
        vop(lambda E: E.tensor_scalar_mul(cb, b["eq1"][:], w1))
        ins = E.scalar_tensor_tensor(cb, b["eq2"][:], w2, cb, ALU.mult,
                                     ALU.add) if w.on("vector") else None
        w.inc("vector", ins, "vlg", 1)

    for g in range(2):
        for i in range(8):
            if g > 0 and i == 0:
                w.wait("vector", "tgu", w.notes.get("tgu_group_end", 0))
                w.wait("vector", "tdn", w.notes.get("tdn_group_end", 0))
            idx = ln_tile_b(g, i)
            router_tile(g, i, idx)

        def slab_aps(slg_):
            g_, r = divmod(slg_, NEXP * 8)
            e_, sl_ = divmod(r, 8)
            wg_ = io["w_gate"][e_] if e_ < 7 else io["sw_gate"][0]
            wu_ = io["w_up"][e_] if e_ < 7 else io["sw_up"][0]
            return wg_, wu_, sl_

        def load_slab(slg_):
            if slg_ >= 2 * NEXP * 8:
                return
            wg_, wu_, sl_ = slab_aps(slg_)
            par = slg_ % 2
            if slg_ >= 2:
                w.wait("gpsimd", "tgu", 8 * (slg_ - 1))
            for kc in range(8):
                ins = E.dma_start(out=b["wg"][par][:, kc*512:(kc+1)*512],
                                  in_=wg_[kc*128:(kc+1)*128, sl_*512:(sl_+1)*512]) \
                    if w.on("gpsimd") else None
                w.inc("gpsimd", ins, "wld%d" % par, 16)
                ins = E.dma_start(out=b["wu"][par][:, kc*512:(kc+1)*512],
                                  in_=wu_[kc*128:(kc+1)*128, sl_*512:(sl_+1)*512]) \
                    if w.on("gpsimd") else None
                w.inc("gpsimd", ins, "wld%d" % par, 16)

        if g == 0:
            load_slab(0)
            load_slab(1)

        for e in range(NEXP):
            wd_ap = io["w_down"][e] if e < 7 else io["sw_down"][0]
            eg = g * NEXP + e
            for hh in range(2):
                for hcl in range(16):
                    hc = hh * 16 + hcl
                    sl = hc // 4
                    slg = eg * 8 + sl
                    par = slg % 2
                    hoff = (hc % 4) * 128
                    if hcl % 4 == 0:
                        load_slab(slg + 2)
                    if hcl == 8:
                        prevdn = w.notes.get("tdn_prev_half")
                        if prevdn is not None:
                            w.wait("gpsimd", "tdn", prevdn)
                        for hcl2 in range(16):
                            ins = E.dma_start(
                                out=b["wd"][:, hcl2*1024:(hcl2+1)*1024],
                                in_=wd_ap[(hh*16+hcl2)*128:(hh*16+hcl2+1)*128, :]) \
                                if w.on("gpsimd") else None
                            w.inc("gpsimd", ins, "wdl", 16)
                    gp = hc % 2
                    w.wait("tensor", "wld%d" % par, 16 * 16 * (slg // 2 + 1))
                    if w.cnt["tgu"] >= 4:
                        w.wait("tensor", "ssil", w.cnt["ssil"] - 1)
                    if hc == 0:
                        w.wait("tensor", "xcd", w.cnt["xcd"])
                    if hcl == 0:
                        w.wait("tensor", "vmx", w.cnt["vmx"])
                    for kc in range(8):
                        for nh in range(2):
                            ins = E.matmul(b["ps_g"][gp][:, nh*512:(nh+1)*512],
                                           b["wg"][par][:, kc*512 + hoff: kc*512 + hoff + 128],
                                           b["xnfm"][:, kc*1024 + nh*512: kc*1024 + (nh+1)*512],
                                           start=(kc == 0), stop=(kc == 7)) \
                                if w.on("tensor") else None
                    w.inc("tensor", ins, "tgu", 1)
                    if hc <= 1:
                        w.wait("tensor", "vlg", w.cnt["vlg"])
                    if w.cnt["tgu"] >= 4:
                        w.wait("tensor", "vh", w.cnt["vh"] - 1)
                    for kc in range(8):
                        for nh in range(2):
                            ins = E.matmul(b["ps_u"][gp][:, nh*512:(nh+1)*512],
                                           b["wu"][par][:, kc*512 + hoff: kc*512 + hoff + 128],
                                           b["xnfm"][:, kc*1024 + nh*512: kc*1024 + (nh+1)*512],
                                           start=(kc == 0), stop=(kc == 7)) \
                                if w.on("tensor") else None
                    w.inc("tensor", ins, "tgu", 1)
                    w.notes[("slabT", slg)] = w.cnt["tgu"]
                    w.wait("scalar", "tgu", w.cnt["tgu"] - 1)
                    w.wait("scalar", "vh", w.cnt["vh"])
                    ins = E.activation(b["tmpg"][:], b["ps_g"][gp][:], AF.Silu) \
                        if w.on("scalar") else None
                    w.inc("scalar", ins, "ssil", 1)
                    w.wait("vector", "tgu", w.cnt["tgu"])
                    w.wait("vector", "ssil", w.cnt["ssil"])
                    prevdn = w.notes.get("tdn_prev_half")
                    if prevdn is not None and hcl == 0:
                        w.wait("vector", "tdn", prevdn)
                    ins = E.tensor_tensor(b["h"][:, hcl*1024:(hcl+1)*1024],
                                          b["tmpg"][:], b["ps_u"][gp][:],
                                          ALU.mult) if w.on("vector") else None
                    w.inc("vector", ins, "vh", 1)

                for ts_ in range(8):
                    dp = ts_ % 2
                    psd = b["ps_g"][dp]
                    w.wait("tensor", "xcd", w.cnt["xcd"])
                    w.wait("tensor", "vh", w.cnt["vh"])
                    w.wait("tensor", "wdl", w.cnt["wdl"])
                    if ts_ <= 1:
                        w.wait("tensor", "ssil", w.cnt["ssil"])
                    if w.cnt["tdn"] >= 2:
                        w.wait("tensor", "vmx", w.cnt["vmx"] - 1)
                    for hcl in range(16):
                        for dh in range(2):
                            ins = E.matmul(psd[:, dh*512:(dh+1)*512],
                                           b["h"][:, hcl*1024 + ts_*128: hcl*1024 + ts_*128 + 128],
                                           b["wd"][:, hcl*1024 + dh*512: hcl*1024 + (dh+1)*512],
                                           start=(hcl == 0), stop=(hcl == 15)) \
                                if w.on("tensor") else None
                    w.inc("tensor", ins, "tdn", 1)
                    w.wait("vector", "tdn", w.cnt["tdn"])
                    w.wait("vector", "vlg", w.cnt["vlg"])
                    mslice = b["mixed"][:, ts_*1024:(ts_+1)*1024]
                    ccol = b["comb"][:, ts_*8 + min(e, 7): ts_*8 + min(e, 7) + 1]
                    if e == 0 and hh == 0:
                        w.wait("vector", "sfin", w.notes.get("sfin_group_end", 0))
                        ins = E.tensor_scalar_mul(mslice, psd[:],
                                                  ccol) if w.on("vector") else None
                    else:
                        ins = E.scalar_tensor_tensor(mslice, psd[:], ccol,
                                                     mslice, ALU.mult, ALU.add) \
                            if w.on("vector") else None
                    w.inc("vector", ins, "vmx", 1)
                w.notes["tdn_prev_half"] = w.cnt["tdn"]
        w.notes["tgu_group_end"] = w.cnt["tgu"]
        w.notes["tdn_group_end"] = w.cnt["tdn"]

        for i in range(8):
            par = i % 2
            gi = g * 8 + i
            if gi >= 2:
                w.wait("sync", "vfin", w.notes.get(("fin", gi - 2), 0))
            ins = E.dma_start(out=b["qt2"][par][:],
                              in_=io["qx"][gi*128:(gi+1)*128, :]) if w.on("sync") else None
            w.inc("sync", ins, "qld%d" % par, 16)
            ins = E.dma_start(out=b["st2"][par][:],
                              in_=io["s0"][gi*128:(gi+1)*128, :]) if w.on("sync") else None
            w.inc("sync", ins, "sld%d" % par, 16)
            w.wait("scalar", "vmx", w.cnt["vmx"])
            if gi >= 2:
                w.wait("scalar", "st%d" % par, 16 * (gi // 2))
            ins = E.activation(b["gbuf"][par][:], b["mixed"][:, i*1024:(i+1)*1024],
                               AF.Silu) if w.on("scalar") else None
            w.inc("scalar", ins, "sfin", 1)
            w.wait("vector", "sfin", w.cnt["sfin"])
            w.wait("vector", "qld%d" % par, w.cnt["qld%d" % par])
            w.wait("vector", "sld%d" % par, w.cnt["sld%d" % par])
            ins = E.tensor_tensor(b["gbuf"][par][:], b["gbuf"][par][:],
                                  b["st2"][par][:], ALU.mult) if w.on("vector") else None
            w.inc("vector", ins, "vself", 1)
            w.wait("vector", "vself", w.cnt["vself"])
            ins = E.tensor_tensor(b["gbuf"][par][:], b["gbuf"][par][:],
                                  b["qt2"][par][:], ALU.add) if w.on("vector") else None
            w.inc("vector", ins, "vfin", 1)
            w.notes[("fin", gi)] = w.cnt["vfin"]
            w.wait("sync", "vfin", w.cnt["vfin"])
            ins = E.dma_start(out=io["x1"][gi*128:(gi+1)*128, :],
                              in_=b["gbuf"][par][:]) if w.on("sync") else None
            w.inc("sync", ins, "st%d" % par, 16)
        w.notes["sfin_group_end"] = w.cnt["sfin"]

    if w.on("sync"):
        E.wait_ge(w.sems["st0"], w.cnt["st0"])
        E.wait_ge(w.sems["st1"], w.cnt["st1"])


def build_phase_b_nc():
    import contextlib
    nc = bass.Bass("TRN2", target_bir_lowering=False, debug=False)
    io = {
        "qx": nc.dram_tensor("qx", [NT, D], F32, kind="ExternalInput").ap(),
        "s0": nc.dram_tensor("s0", [NT, D], F32, kind="ExternalInput").ap(),
        "router_w": nc.dram_tensor("router_w", [7, D], F32, kind="ExternalInput").ap(),
        "w_gate": nc.dram_tensor("w_gate", [7, D, 4096], F32, kind="ExternalInput").ap(),
        "w_up": nc.dram_tensor("w_up", [7, D, 4096], F32, kind="ExternalInput").ap(),
        "w_down": nc.dram_tensor("w_down", [7, 4096, D], F32, kind="ExternalInput").ap(),
        "sw_gate": nc.dram_tensor("sw_gate", [1, D, 4096], F32, kind="ExternalInput").ap(),
        "sw_up": nc.dram_tensor("sw_up", [1, D, 4096], F32, kind="ExternalInput").ap(),
        "sw_down": nc.dram_tensor("sw_down", [1, 4096, D], F32, kind="ExternalInput").ap(),
        "identb": nc.dram_tensor("identb", [128, 128], BF16, kind="ExternalInput").ap(),
        "identf": nc.dram_tensor("identf", [128, 128], F32, kind="ExternalInput").ap(),
        "x1": nc.dram_tensor("x1", [NT, D], F32, kind="ExternalOutput").ap(),
    }
    import contextlib
    with contextlib.ExitStack() as ctx:
        bufs = phase_b_bufs(nc, ctx)
        sems = {s: ctx.enter_context(nc.semaphore("sB_" + s)) for s in PHASE_B_SEMS}
        with nc.Block() as blk:
            @blk.sync
            def _(E):
                phase_b_plan(Walk("sync", E, sems), bufs, io)
            @blk.gpsimd
            def _(E):
                phase_b_plan(Walk("gpsimd", E, sems), bufs, io)
            @blk.tensor
            def _(E):
                phase_b_plan(Walk("tensor", E, sems), bufs, io)
            @blk.scalar
            def _(E):
                phase_b_plan(Walk("scalar", E, sems), bufs, io)
            @blk.vector
            def _(E):
                phase_b_plan(Walk("vector", E, sems), bufs, io)
    return nc



PHASE_C_SEMS = ["cst", "kbd", "wld0", "wld1", "wtp", "wcp", "xld0", "xld1",
                "sqd", "lnd", "xtd", "xcd", "tpj", "vpe", "spe",
                "tst", "vst", "ccd", "ccc", "gld0", "gld1", "vpr",
                "tkt", "skt", "tat", "vat", "tnm", "vnm", "tup", "vup", "srec",
                "vrec",
                "tao", "vao", "sfin", "vfin", "st0", "st1", "vself", "sself"]

H = 16
DH = 64
STL = 66560

def phase_c_bufs(nc, ctx):
    ent = ctx.enter_context
    bufs = dict(
        wqk=ent(nc.sbuf_tensor("c_wqk", [128, 16384], BF16)),
        wvT=ent(nc.sbuf_tensor("c_wvT", [128, 8192], BF16)),
        woT=ent(nc.sbuf_tensor("c_woT", [128, 8192], BF16)),
        qfm=ent(nc.sbuf_tensor("c_qfm", [128, 16384], BF16)),
        kfm=ent(nc.sbuf_tensor("c_kfm", [128, 16384], BF16)),
        vtm=ent(nc.sbuf_tensor("c_vtm", [128, 16384], BF16)),
        xnc=ent(nc.sbuf_tensor("c_xnc", [128, 8192], BF16)),
        xt0=ent(nc.sbuf_tensor("c_xt0", [128, 1024], F32)),
        wtmp=[ent(nc.sbuf_tensor(f"c_wtmp{i}", [128, 1024], BF16)) for i in range(2)],
        e1=ent(nc.sbuf_tensor("c_e1", [128, 512], F32)),
        e2=ent(nc.sbuf_tensor("c_e2", [128, 512], BF16)),
        kvloc=ent(nc.sbuf_tensor("c_kvloc", [64, 2080], F32)),
        stpack=ent(nc.sbuf_tensor("c_stpack", [64, 1040], F32)),
        gbf=ent(nc.sbuf_tensor("c_gbf", [64, 1040], F32)),
        kvbf0=ent(nc.sbuf_tensor("c_kvbf0", [128, 1040], BF16)),
        selA=ent(nc.sbuf_tensor("c_selA", [64, 16], F32)),
        selB=ent(nc.sbuf_tensor("c_selB", [64, 16], F32)),
        cmask=ent(nc.sbuf_tensor("c_cmask", [128, 128], BF16)),
        onesb=ent(nc.sbuf_tensor("c_onesb", [128, 1], BF16)),
        ktm=[ent(nc.sbuf_tensor(f"c_ktm{i}", [128, 64], BF16)) for i in range(2)],
        at=[ent(nc.sbuf_tensor(f"c_at{i}", [128, 128], BF16)) for i in range(2)],
        identb=ent(nc.sbuf_tensor("c_identb", [128, 128], BF16)),
        lncol=ent(nc.sbuf_tensor("c_lncol", [128, 16], F32)),
        epscol=ent(nc.sbuf_tensor("c_epscol", [128, 2], F32)),
        rcol=ent(nc.sbuf_tensor("c_rcol", [128, 4], F32)),
        s1o=ent(nc.sbuf_tensor("c_s1o", [128, 256], F32)),
        ps_pj=[ent(nc.psum_tensor(f"c_pspj{i}", [128, 512], F32)) for i in range(2)],
        ps_xnt=ent(nc.psum_tensor("c_psxnt", [128, 1024], BF16)),
        ps_at=[ent(nc.psum_tensor(f"c_psat{i}", [128, 128], F32)) for i in range(2)],
        ps_kt=[ent(nc.psum_tensor(f"c_pskt{i}", [128, 64], BF16)) for i in range(2)],
    )
    bufs["wqT"] = bufs["wqk"]
    bufs["wkT"] = bufs["wqk"]
    bufs["xnbf"] = bufs["wtmp"]
    a_kvl = nc.lookup_mloc(bufs["kvloc"]).addr
    bufs["xt"] = [bufs["xt0"],
                  nc.alloc_sbuf_tensor_at("c_xt1", [128, 1024], F32, offset=a_kvl)]
    bufs["s1o2"] = nc.alloc_sbuf_tensor_at("c_s1o2", [128, 256], F32, offset=a_kvl + 4096)
    bufs["aof"] = bufs["wtmp"][0]
    bufs["sqs"] = bufs["e1"]
    bufs["ps_f"] = [bufs["ps_pj"][i].ap()[:, 0:65] for i in range(2)]
    bufs["ps_st"] = [bufs["ps_pj"][i].ap()[0:64, 256:321] for i in range(2)]
    a_gbf = nc.lookup_mloc(bufs["gbf"]).addr
    bufs["kvbf"] = [bufs["kvbf0"],
                    nc.alloc_sbuf_tensor_at("c_kvbf1", [128, 1040], BF16, offset=a_gbf)]
    return bufs


def phase_c_plan(w: Walk, bufs, io):
    E = w.E
    b = bufs
    ts = bass.ts

    for nm in ("identb", "selA", "selB", "cmask"):
        ins = E.dma_start(out=b[nm][:], in_=io[nm]) if w.on("sync") else None
        w.inc("sync", ins, "cst", 16)
    if w.on("vector"):
        E.memset(b["epscol"][:, 0:1], 1e-5)
        E.memset(b["epscol"][:, 1:2], 1e-8)
        ins = E.memset(b["onesb"][:], 1.0)
    else:
        ins = None
    w.inc("vector", ins, "lnd", 1)

    wmats = [("wq1", b["wqk"], 0), ("wk1", b["wqk"], 8192),
             ("wv1", b["wvT"], 0), ("wo1", b["woT"], 0)]
    for s in range(32):
        mat, wT, wbase = wmats[s // 8]
        mc = s % 8
        par = s % 2
        if s >= 2:
            w.wait("gpsimd", "wtp", 8 * (s - 1))
        ins = E.dma_start(out=b["wtmp"][par][:],
                          in_=io[mat][mc*128:(mc+1)*128, :]) if w.on("gpsimd") else None
        w.inc("gpsimd", ins, "wld%d" % par, 16)
        if s == 0:
            w.wait("tensor", "cst", 64)
        w.wait("tensor", "wld%d" % par, 16 * (s // 2 + 1))
        if s >= 1:
            w.wait("tensor", "wcp", s)
        for kc in range(8):
            ins = E.transpose(b["ps_xnt"][:, ts(kc, 128)],
                              b["wtmp"][par][:, ts(kc, 128)],
                              b["identb"][:]) if w.on("tensor") else None
            w.inc("tensor", ins, "wtp", 1)
        w.wait("vector", "wtp", 8 * (s + 1))
        if w.on("vector"):
            dst = wT.ap()[:, wbase:wbase+8192].rearrange(
                "p (k f) -> p k f", k=8)[:, :, mc*128:(mc+1)*128]
            src = b["ps_xnt"].ap().rearrange("p (k f) -> p k f", k=8)
            ins = E.tensor_copy(dst, src)
        else:
            ins = None
        w.inc("vector", ins, "wcp", 1)

    tile_i = 0

    def ln_tile_c(cc, i):
        nonlocal tile_i
        idx = tile_i; tile_i += 1
        par = idx % 2
        cs = 8 * par
        if idx >= 2:
            w.wait("sync", "sqd", 3 * (idx - 2) + 2)
            w.wait("sync", "lnd", (idx - 2) + 2)
        xb = b["xt"][par]
        ins = E.dma_start(out=xb[:],
                          in_=io["x1"][(cc*8+i)*128:(cc*8+i+1)*128, :]) if w.on("sync") else None
        w.inc("sync", ins, "xld%d" % par, 16)
        w.wait("scalar", "xld%d" % par, w.cnt["xld%d" % par])
        w.wait("scalar", "vpe", w.cnt["vpe"])
        ssqa = b["lncol"][:, cs+1:cs+2]
        ssqb = b["lncol"][:, cs+7:cs+8]
        ins = E.activation(b["sqs"][:], xb[:, 0:512], AF.Square,
                           accum_out=ssqa) if w.on("scalar") else None
        w.inc("scalar", ins, "sself", 1)
        w.wait("scalar", "sself", w.cnt["sself"])
        ins = E.activation(b["sqs"][:], xb[:, 512:1024], AF.Square,
                           accum_out=ssqb) if w.on("scalar") else None
        w.inc("scalar", ins, "sqd", 2)
        w.wait("vector", "xld%d" % par, w.cnt["xld%d" % par])
        ssum = b["lncol"][:, cs:cs+1]
        mean = b["lncol"][:, cs+2:cs+3]
        var = b["lncol"][:, cs+3:cs+4]
        m2 = b["lncol"][:, cs+4:cs+5]
        ins = E.reduce_sum(ssum, xb[:], axis=AX.X) if w.on("vector") else None
        w.inc("vector", ins, "vself", 1)
        w.wait("vector", "sqd", w.cnt["sqd"])
        w.wait("vector", "vself", w.cnt["vself"])
        ins = E.tensor_scalar_mul(mean, ssum, 1.0 / D) if w.on("vector") else None
        w.inc("vector", ins, "vself", 1)
        w.wait("vector", "vself", w.cnt["vself"])
        ins = E.tensor_tensor(m2, mean, mean, ALU.mult) if w.on("vector") else None
        w.inc("vector", ins, "vself", 1)
        w.wait("vector", "vself", w.cnt["vself"])
        ins = E.tensor_add(ssqa, ssqa, ssqb) if w.on("vector") else None
        w.inc("vector", ins, "vself", 1)
        w.wait("vector", "vself", w.cnt["vself"])
        ins = E.scalar_tensor_tensor(var, ssqa, 1.0 / D, m2, ALU.mult,
                                     ALU.subtract) if w.on("vector") else None
        w.inc("vector", ins, "xtd", 1)
        w.wait("scalar", "xtd", w.cnt["xtd"])
        lnv = b["lncol"][:, cs+5:cs+6]
        rstd = b["lncol"][:, cs+6:cs+7]
        if idx == 0:
            w.wait("scalar", "lnd", 1)
        ins = E.activation(lnv, var, AF.Ln,
                           bias=b["epscol"][:, 0:1]) if w.on("scalar") else None
        w.inc("scalar", ins, "sself", 1)
        w.wait("scalar", "sself", w.cnt["sself"])
        ins = E.activation(rstd, lnv, AF.Exp, scale=-0.5) if w.on("scalar") else None
        w.inc("scalar", ins, "sqd", 1)
        w.wait("vector", "sqd", w.cnt["sqd"])
        w.wait("vector", "xtd", w.cnt["xtd"])
        prev = w.notes.get(("lnT", idx - 2))
        if prev is not None:
            w.wait("vector", "xtd", prev)
        ins = E.tensor_scalar(b["xnbf"][par][:], xb[:], mean, rstd,
                              ALU.subtract, ALU.mult) if w.on("vector") else None
        w.inc("vector", ins, "lnd", 1)
        w.wait("tensor", "lnd", w.cnt["lnd"])
        w.wait("tensor", "wcp", 32)
        if idx >= 1:
            w.wait("tensor", "xcd", idx)
        for cq in range(8):
            ins = E.transpose(b["ps_xnt"][:, ts(cq, 128)], b["xnbf"][par][:, ts(cq, 128)],
                              b["identb"][:]) if w.on("tensor") else None
            w.inc("tensor", ins, "xtd", 1)
        w.notes[("lnT", idx)] = w.cnt["xtd"]
        w.wait("vector", "xtd", w.cnt["xtd"])
        tp = w.notes.get("tpj_chunk_done")
        if tp:
            w.wait("vector", "tpj", tp)
        if w.on("vector"):
            dst = b["xnc"].ap().rearrange("p (k f) -> p k f", k=8)[:, :, i*128:(i+1)*128]
            src = b["ps_xnt"].ap().rearrange("p (k f) -> p k f", k=8)
            ins = E.tensor_copy(dst, src)
        else:
            ins = None
        w.inc("vector", ins, "xcd", 1)

    def elu_evac(gi, ps, dst_ap):
        w.wait("vector", "tpj", w.cnt["tpj"])
        w.wait("vector", "spe", w.cnt["spe"])
        w.wait("vector", "sqd", w.cnt["sqd"])
        ins = E.tensor_scalar_min(b["e1"][:], ps[:], 0.0) if w.on("vector") else None
        w.inc("vector", ins, "vself", 1)
        w.wait("scalar", "vself", w.cnt["vself"])
        ins = E.activation(b["e2"][:], b["e1"][:], AF.Exp) if w.on("scalar") else None
        w.inc("scalar", ins, "spe", 1)
        w.wait("vector", "spe", w.cnt["spe"])
        ins = E.scalar_tensor_tensor(dst_ap, ps[:], 0.0, b["e2"][:],
                                     ALU.max, ALU.add) if w.on("vector") else None
        w.inc("vector", ins, "vpe", 1)

    def copy_evac(gi, ps, dst_ap):
        w.wait("vector", "tpj", w.cnt["tpj"])
        ins = E.tensor_copy(dst_ap, ps[:]) if w.on("vector") else None
        w.inc("vector", ins, "vpe", 1)

    for cc in range(2):
        for i in range(8):
            ln_tile_c(cc, i)
        for proj, wT, wbase, kind in (("q", b["wqk"], 0, "elu_fm"),
                                      ("k", b["wqk"], 8192, "elu_fm"),
                                      ("v", b["wvT"], 0, "tm")):
            for gi in range(16):
                par = w.cnt["tpj"] % 2
                ps = b["ps_pj"][par]
                if gi == 0:
                    w.wait("tensor", "xcd", w.cnt["xcd"])
                if w.cnt["tpj"] >= 2:
                    w.wait("tensor", "vpe", w.cnt["vpe"] - 1)
                if kind == "tm":
                    tt, ch = gi // 2, gi % 2
                    for kc in range(8):
                        ins = E.matmul(ps[:], b["xnc"][:, kc*1024 + tt*128: kc*1024 + (tt+1)*128],
                                       wT[:, wbase + kc*1024 + ch*512: wbase + kc*1024 + (ch+1)*512],
                                       start=(kc == 0), stop=(kc == 7)) \
                            if w.on("tensor") else None
                else:
                    mc, th = gi // 2, gi % 2
                    for kc in range(8):
                        ins = E.matmul(ps[:], wT[:, wbase + kc*1024 + mc*128: wbase + kc*1024 + (mc+1)*128],
                                       b["xnc"][:, kc*1024 + th*512: kc*1024 + (th+1)*512],
                                       start=(kc == 0), stop=(kc == 7)) \
                            if w.on("tensor") else None
                w.inc("tensor", ins, "tpj", 1)
                if kind == "tm":
                    tt, ch = gi // 2, gi % 2
                    dst = b["vtm"][:, (cc*8+tt)*1024 + ch*512: (cc*8+tt)*1024 + (ch+1)*512]
                    copy_evac(gi, ps, dst)
                else:
                    mc, th = gi // 2, gi % 2
                    dstbuf = b["qfm"] if proj == "q" else b["kfm"]
                    dst = dstbuf[:, mc*2048 + cc*1024 + th*512: mc*2048 + cc*1024 + (th+1)*512]
                    elu_evac(gi, ps, dst)
        w.notes["tpj_chunk_done"] = w.cnt["tpj"]

    def ktm_make(cc, s, h, par, mmfree=None):
        pb = 64 * (h % 2)
        col = (h // 2) * 2048 + cc * 1024 + s * 128
        w.wait("tensor", "vpe", w.cnt["vpe"])
        if w.cnt["tkt"] >= 2:
            w.wait("tensor", "skt", w.cnt["tkt"] - 1)
        ins = E.transpose(b["ps_kt"][par][:, 0:64], b["kfm"][pb:pb+64, col:col+128],
                          b["identb"][pb:pb+64, pb:pb+64]) if w.on("tensor") else None
        w.inc("tensor", ins, "tkt", 1)
        w.wait("scalar", "tkt", w.cnt["tkt"])
        if mmfree is not None:
            w.wait("scalar", mmfree[0], mmfree[1])
        ins = E.copy(b["ktm"][par][:], b["ps_kt"][par][:, 0:64]) if w.on("scalar") else None
        w.inc("scalar", ins, "skt", 1)

    kti = 0
    for cc in range(2):
        for h in range(H):
            hpar = (cc * H + h) % 2
            for s in range(8):
                par = kti % 2
                ktm_make(cc, s, h, par, ("tst", max(0, w.cnt["tst"] - 1)))
                kti += 1
                w.wait("tensor", "skt", w.cnt["skt"])
                if s == 0 and w.cnt["vst"] >= 1:
                    w.wait("tensor", "vst", w.cnt["vst"] - 1)
                vcol = (cc*8+s)*1024 + h*64
                ins = E.matmul(b["ps_st"][hpar][:, 0:64], b["ktm"][par][:],
                               b["vtm"][:, vcol:vcol+64],
                               start=(s == 0), stop=False) if w.on("tensor") else None
                ins = E.matmul(b["ps_st"][hpar][:, 64:65], b["ktm"][par][:],
                               b["onesb"][:], start=False,
                               stop=(s == 7)) if w.on("tensor") else None
                w.inc("tensor", ins, "tst", 1)
            w.wait("vector", "tst", w.cnt["tst"])
            if w.on("vector"):
                E.tensor_copy(b["kvloc"][:, cc*1040 + h*64: cc*1040 + (h+1)*64],
                              b["ps_st"][hpar][:, 0:64])
                ins = E.tensor_copy(b["kvloc"][:, cc*1040 + 1024 + h: cc*1040 + 1024 + h + 1],
                                    b["ps_st"][hpar][:, 64:65])
            else:
                ins = None
            w.inc("vector", ins, "vst", 1)

    w.wait("gpsimd", "vst", w.cnt["vst"])
    for cc in range(2):
        cv = io["cc_in"][cc, 0:66560].rearrange("(p f) -> p f", p=64)
        for hf in range(2):
            ins = E.dma_start(
                out=cv[:, hf*520:(hf+1)*520],
                in_=b["kvloc"][:, cc*1040 + hf*520: cc*1040 + (hf+1)*520]) \
                if w.on("gpsimd") else None
            w.inc("gpsimd", ins, "ccd", 16)
    w.wait("gpsimd", "ccd", w.cnt["ccd"])
    ins = E.collective_compute(
        "AllGather", ALU.bypass, replica_groups=[[0, 1, 2, 3, 4, 5, 6, 7]],
        ins=[io["cc_in"].opt()], outs=[io["cc_out"].opt()],
    ) if w.on("gpsimd") else None
    w.inc("gpsimd", ins, "ccc", 1)

    w.wait("vector", "ccd", w.cnt["ccd"])
    ins = E.memset(b["kvloc"][:], 0.0) if w.on("vector") else None
    w.inc("vector", ins, "vself", 1)
    w.wait("vector", "vself", w.cnt["vself"])
    for j in range(16):
        w.wait("sync", "ccc", w.cnt["ccc"])
        if j >= 1:
            w.wait("sync", "vpr", w.cnt["vpr"])
        gv = io["cc_out"][j, 0:66560].rearrange("(p f) -> p f", p=64)
        for hf in range(2):
            ins = E.dma_start(out=b["gbf"][:, hf*520:(hf+1)*520],
                              in_=gv[:, hf*520:(hf+1)*520]) if w.on("sync") else None
            w.inc("sync", ins, "gld0", 16)
        w.wait("vector", "gld0", w.cnt["gld0"])
        ins = E.scalar_tensor_tensor(b["kvloc"][:, 0:1040], b["gbf"][:],
                                     b["selA"][:, j:j+1], b["kvloc"][:, 0:1040],
                                     ALU.mult, ALU.add) if w.on("vector") else None
        w.inc("vector", ins, "vself", 1)
        w.wait("vector", "vself", w.cnt["vself"])
        ins = E.scalar_tensor_tensor(b["kvloc"][:, 1040:2080], b["gbf"][:],
                                     b["selB"][:, j:j+1], b["kvloc"][:, 1040:2080],
                                     ALU.mult, ALU.add) if w.on("vector") else None
        w.inc("vector", ins, "vpr", 1)

    ao = b["wqk"]
    fi = 0
    for cc in range(2):
        w.wait("vector", "vpr", w.cnt["vpr"])
        w.wait("vector", "vup", w.cnt["vup"])
        w.wait("vector", "tnm", w.cnt["tnm"])
        w.wait("vector", "kbd", w.cnt["kbd"])
        if w.on("vector"):
            E.tensor_copy(b["stpack"][:], b["kvloc"][:, cc*1040:(cc+1)*1040])
            ins = E.tensor_copy(b["kvbf"][0][0:64, :], b["kvloc"][:, cc*1040:(cc+1)*1040])
        else:
            ins = None
        w.inc("vector", ins, "vup", 1)
        w.wait("sync", "vup", w.cnt["vup"])
        ins = E.dma_start(out=b["kvbf"][0][64:128, :],
                          in_=b["kvbf"][0][0:64, :]) if w.on("sync") else None
        w.inc("sync", ins, "kbd", 16)
        w.notes[("kbd", cc, 0)] = w.cnt["kbd"]
        for s in range(8):
            tl = cc * 8 + s
            sb = s % 2
            if s < 7:
                for h in range(H):
                    par = kti % 2
                    ktm_make(cc, s, h, par, ("tup", max(0, w.cnt["tup"] - 1)))
                    kti += 1
                    w.wait("tensor", "skt", w.cnt["skt"])
                    upar = w.cnt["tup"] % 2
                    if w.cnt["tup"] >= 2:
                        w.wait("tensor", "vup", w.cnt["vup"] - 1)
                    vcol = tl * 1024 + h * 64
                    if w.on("tensor"):
                        E.matmul(b["ps_st"][upar][:, 0:64], b["ktm"][par][:],
                                 b["vtm"][:, vcol:vcol+64], start=True, stop=False)
                        ins = E.matmul(b["ps_st"][upar][:, 64:65], b["ktm"][par][:],
                                       b["onesb"][:], start=False, stop=True)
                    else:
                        ins = None
                    w.inc("tensor", ins, "tup", 1)
                    w.wait("vector", "tup", w.cnt["tup"])
                    if w.on("vector"):
                        E.tensor_add(b["stpack"][:, h*64:(h+1)*64],
                                     b["stpack"][:, h*64:(h+1)*64],
                                     b["ps_st"][upar][:, 0:64])
                        ins = E.tensor_add(b["stpack"][:, 1024+h:1024+h+1],
                                           b["stpack"][:, 1024+h:1024+h+1],
                                           b["ps_st"][upar][:, 64:65])
                    else:
                        ins = None
                    w.inc("vector", ins, "vup", 1)
                w.wait("vector", "vup", w.cnt["vup"])
                prevr = w.notes.get(("fwdend", cc, s - 1),
                                    w.notes.get(("fwdend", cc - 1, 7), 0))
                w.wait("vector", "tnm", prevr)
                w.wait("vector", "kbd", w.cnt["kbd"])
                ins = E.tensor_copy(b["kvbf"][1-sb][0:64, :],
                                    b["stpack"][:]) if w.on("vector") else None
                w.inc("vector", ins, "vup", 1)
                w.wait("sync", "vup", w.cnt["vup"])
                ins = E.dma_start(out=b["kvbf"][1-sb][64:128, :],
                                  in_=b["kvbf"][1-sb][0:64, :]) if w.on("sync") else None
                w.inc("sync", ins, "kbd", 16)
                w.notes[("kbd", cc, s + 1)] = w.cnt["kbd"]
            def fwd_num(fj):
                parj, hj = fj % 2, fj - (cc * 8 + s) * H
                pbj = 64 * (hj % 2)
                qcolj = (hj // 2) * 2048 + cc * 1024 + s * 128
                w.wait("tensor", "vat", fj + 1)
                w.wait("tensor", "kbd", w.notes[("kbd", cc, s)])
                if fj >= 2:
                    w.wait("tensor", "vnm", fj - 1)
                vcolj = tl * 1024 + hj * 64
                kb = b["kvbf"][sb]
                if w.on("tensor"):
                    E.matmul(b["ps_f"][parj][:, 0:64], b["at"][parj][:],
                             b["vtm"][:, vcolj:vcolj+64], start=True, stop=False)
                    E.matmul(b["ps_f"][parj][:, 0:64], b["qfm"][pbj:pbj+64, qcolj:qcolj+128],
                             kb[pbj:pbj+64, hj*64:(hj+1)*64], start=False, stop=False)
                    E.matmul(b["ps_f"][parj][:, 64:65], b["at"][parj][:],
                             b["onesb"][:], start=False, stop=False)
                    ins = E.matmul(b["ps_f"][parj][:, 64:65],
                                   b["qfm"][pbj:pbj+64, qcolj:qcolj+128],
                                   kb[pbj:pbj+64, 1024+hj:1024+hj+1],
                                   start=False, stop=True)
                else:
                    ins = None
                w.inc("tensor", ins, "tnm", 1)
                w.wait("scalar", "tnm", w.cnt["tnm"])
                if fj >= 2:
                    w.wait("scalar", "vnm", fj - 1)
                ins = E.activation(b["rcol"][:, parj:parj+1], b["ps_f"][parj][:, 64:65],
                                   AF.Copy, bias=1e-8) if w.on("scalar") else None
                w.inc("scalar", ins, "srec", 1)
                w.wait("vector", "srec", w.cnt["srec"])
                if fj >= 2:
                    w.wait("vector", "vnm", fj - 1)
                ins = E.reciprocal(b["rcol"][:, 2+parj:3+parj],
                                   b["rcol"][:, parj:parj+1]) if w.on("vector") else None
                w.inc("vector", ins, "vrec", 1)
                w.wait("scalar", "vrec", w.cnt["vrec"])
                ins = E.activation(ao[:, tl*1024 + hj*64: tl*1024 + (hj+1)*64],
                                   b["ps_f"][parj][:, 0:64], AF.Copy,
                                   scale=b["rcol"][:, 2+parj:3+parj]) if w.on("scalar") else None
                w.inc("scalar", ins, "vnm", 1)

            for h in range(H):
                par = fi % 2
                pb = 64 * (h % 2)
                qcol = (h // 2) * 2048 + cc * 1024 + s * 128
                if fi >= 2:
                    w.wait("tensor", "vat", fi - 1)
                if fi == 0:
                    w.wait("tensor", "vpe", w.cnt["vpe"])
                ins = E.matmul(b["ps_at"][par][:], b["kfm"][pb:pb+64, qcol:qcol+128],
                               b["qfm"][pb:pb+64, qcol:qcol+128],
                               start=True, stop=True) if w.on("tensor") else None
                w.inc("tensor", ins, "tat", 1)
                w.wait("vector", "tat", w.cnt["tat"])
                if fi >= 2:
                    w.wait("vector", "tnm", fi - 1)
                ins = E.tensor_tensor(b["at"][par][:], b["ps_at"][par][:], b["cmask"][:],
                                      ALU.mult) if w.on("vector") else None
                w.inc("vector", ins, "vat", 1)
                if h >= 1:
                    fwd_num(fi - 1)
                fi += 1
            fwd_num(fi - 1)
            w.notes[("fwdend", cc, s)] = w.cnt["tnm"]

    for tl in range(16):
        w.wait("tensor", "vnm", w.cnt["vnm"])
        if tl >= 1:
            w.wait("tensor", "vao", w.cnt["vao"])
        for kc in range(8):
            ins = E.transpose(b["ps_xnt"][:, ts(kc, 128)],
                              ao[:, tl*1024 + kc*128: tl*1024 + (kc+1)*128],
                              b["identb"][:]) if w.on("tensor") else None
            w.inc("tensor", ins, "tao", 1)
        w.wait("vector", "tao", w.cnt["tao"])
        if tl >= 1:
            w.wait("vector", "tpj", w.cnt["tpj"])
        ins = E.tensor_copy(b["aof"][:], b["ps_xnt"][:]) if w.on("vector") else None
        w.inc("vector", ins, "vao", 1)
        w.wait("tensor", "vao", w.cnt["vao"])
        for fh in range(2):
            pspj = b["ps_pj"][fh]
            if tl >= 1:
                w.wait("tensor", "sfin", w.cnt["sfin"])
            for kc in range(8):
                ins = E.matmul(pspj[:], b["aof"][:, kc*128:(kc+1)*128],
                               b["woT"][:, kc*1024 + fh*512: kc*1024 + (fh+1)*512],
                               start=(kc == 0), stop=(kc == 7)) if w.on("tensor") else None
            w.inc("tensor", ins, "tpj", 1)
        w.wait("scalar", "tpj", w.cnt["tpj"])
        for fq in range(4):
            qi = 4 * tl + fq
            sb_ = b["s1o"] if fq % 2 == 0 else b["s1o2"]
            if qi >= 2:
                w.wait("scalar", "st0", 16 * (qi - 1))
            ins = E.copy(sb_[:], b["ps_pj"][fq // 2][:, (fq % 2)*256:(fq % 2)*256+256]) \
                if w.on("scalar") else None
            w.inc("scalar", ins, "sfin", 1)
            w.wait("sync", "sfin", w.cnt["sfin"])
            ins = E.dma_start(out=io["s1"][tl*128:(tl+1)*128, fq*256:(fq+1)*256],
                              in_=sb_[:]) if w.on("sync") else None
            w.inc("sync", ins, "st0", 16)
    if w.on("sync"):
        E.wait_ge(w.sems["st0"], w.cnt["st0"])


def build_phase_c_nc():
    import contextlib
    nc = bass.Bass("TRN2", target_bir_lowering=False, debug=False)
    io = {
        "x1": nc.dram_tensor("x1", [NT, D], F32, kind="ExternalInput").ap(),
        "wq1": nc.dram_tensor("wq1", [D, D], F32, kind="ExternalInput").ap(),
        "wk1": nc.dram_tensor("wk1", [D, D], F32, kind="ExternalInput").ap(),
        "wv1": nc.dram_tensor("wv1", [D, D], F32, kind="ExternalInput").ap(),
        "wo1": nc.dram_tensor("wo1", [D, D], F32, kind="ExternalInput").ap(),
        "identb": nc.dram_tensor("identb", [128, 128], BF16, kind="ExternalInput").ap(),
        "selA": nc.dram_tensor("selA", [64, 16], F32, kind="ExternalInput").ap(),
        "selB": nc.dram_tensor("selB", [64, 16], F32, kind="ExternalInput").ap(),
        "cmask": nc.dram_tensor("cmask", [128, 128], BF16, kind="ExternalInput").ap(),
        "cc_in": nc.dram_tensor("cc_in", [2, 66560], F32, kind="Internal").ap(),
        "cc_out": nc.dram_tensor("cc_out", [16, 66560], F32, kind="Internal").ap(),
        "s1": nc.dram_tensor("s1", [NT, D], F32, kind="ExternalOutput").ap(),
    }
    with contextlib.ExitStack() as ctx:
        bufs = phase_c_bufs(nc, ctx)
        sems = {s: ctx.enter_context(nc.semaphore("sC_" + s)) for s in PHASE_C_SEMS}
        with nc.Block() as blk:
            @blk.sync
            def _(E):
                phase_c_plan(Walk("sync", E, sems), bufs, io)
            @blk.gpsimd
            def _(E):
                phase_c_plan(Walk("gpsimd", E, sems), bufs, io)
            @blk.tensor
            def _(E):
                phase_c_plan(Walk("tensor", E, sems), bufs, io)
            @blk.scalar
            def _(E):
                phase_c_plan(Walk("scalar", E, sems), bufs, io)
            @blk.vector
            def _(E):
                phase_c_plan(Walk("vector", E, sems), bufs, io)
    return nc


def host_phase_c_inputs(c):
    b_, ca, cb = core_chunks(c)
    selA = np.zeros((64, 16), np.float32)
    selB = np.zeros((64, 16), np.float32)
    for r in range(8):
        br, pr = r // 4, r % 4
        if br != b_:
            continue
        for s in range(2):
            ch = pr if s == 0 else 7 - pr
            if ch < ca:
                selA[:, r*2+s] = 1.0
            if ch < cb:
                selB[:, r*2+s] = 1.0
    cmask = (np.arange(128)[:, None] <= np.arange(128)[None, :]).astype(ml_dtypes.bfloat16)
    return {"selA": selA, "selB": selB, "cmask": cmask,
            "identb": np.eye(128, dtype=ml_dtypes.bfloat16)}



PHASE_D_SEMS = ["cst", "wld0", "wld1", "sab", "vab", "tred", "sbc",
                "tsg", "ssg", "vwt", "twt", "vwc",
                "xld0", "xld1", "sqd", "lnd", "xtd", "xcd", "tup", "sge", "vsq",
                "grc",
                "thf", "vhf", "tdn", "sdn", "vfin", "sld0", "yld0",
                "st0", "st1", "vself", "sself"]

def phase_d_bufs(nc, ctx):
    ent = ctx.enter_context
    bufs = dict(
        wupT=ent(nc.sbuf_tensor("d_wupT", [128, 32768], BF16)),
        wdT=ent(nc.sbuf_tensor("d_wdT", [128, 32768], BF16)),
        xfm4=ent(nc.sbuf_tensor("d_xfm4", [128, 4096], BF16)),
        h4=ent(nc.sbuf_tensor("d_h4", [128, 16384], BF16)),
        hfm=ent(nc.sbuf_tensor("d_hfm", [128, 4096], BF16)),
        xt0=ent(nc.sbuf_tensor("d_xt0", [128, 1024], F32)),
        xn32=ent(nc.sbuf_tensor("d_xn32", [128, 1024], F32)),
        e1=ent(nc.sbuf_tensor("d_e1", [128, 512], F32)),
        s1t=[ent(nc.sbuf_tensor(f"d_s1t{i}", [128, 1024], F32)) for i in range(2)],
        x1t=[ent(nc.sbuf_tensor(f"d_x1t{i}", [128, 1024], F32)) for i in range(2)],
        gsil=ent(nc.sbuf_tensor("d_gsil", [128, 1024], F32)),
        identb=ent(nc.sbuf_tensor("d_identb", [128, 128], BF16)),
        ones128=ent(nc.sbuf_tensor("d_ones128", [128, 1], F32)),
        ones1=ent(nc.sbuf_tensor("d_ones1", [1, 128], F32)),
        acol=ent(nc.sbuf_tensor("d_acol", [128, 4], F32)),
        alpha1=ent(nc.sbuf_tensor("d_alpha1", [1, 1], F32)),
        cvec=ent(nc.sbuf_tensor("d_cvec", [128, 2], F32)),
        ncvec=ent(nc.sbuf_tensor("d_ncvec", [128, 2], F32)),
        lncol=ent(nc.sbuf_tensor("d_lncol", [128, 64], F32)),
        epscol=ent(nc.sbuf_tensor("d_epscol", [128, 2], F32)),
        ps_pj=[ent(nc.psum_tensor(f"d_pspj{i}", [128, 512], F32)) for i in range(2)],
        ps_xnt=ent(nc.psum_tensor("d_psxnt", [128, 1024], BF16)),
        ps_sm=ent(nc.psum_tensor("d_pssm", [128, 512], F32)),
    )
    a_h4 = nc.lookup_mloc(bufs["h4"]).addr
    a_hfm = nc.lookup_mloc(bufs["hfm"]).addr
    a_xfm = nc.lookup_mloc(bufs["xfm4"]).addr
    a_gs = nc.lookup_mloc(bufs["gsil"]).addr
    bufs["wprep"] = nc.alloc_sbuf_tensor_at("d_wprep", [128, 4096], F32, offset=a_h4)
    bufs["sgn2"] = nc.alloc_sbuf_tensor_at("d_sgn2", [128, 4096], BF16, offset=a_h4 + 16384)
    bufs["sgn1"] = nc.alloc_sbuf_tensor_at("d_sgn1", [128, 4096], BF16, offset=a_hfm)
    bufs["wt2"] = nc.alloc_sbuf_tensor_at("d_wt2", [128, 4096], BF16, offset=a_xfm)
    bufs["absb"] = nc.alloc_sbuf_tensor_at("d_absb", [128, 1024], F32, offset=a_gs)
    bufs["xt"] = [bufs["xt0"], bufs["xt0"]]
    return bufs


def phase_d_plan(w: Walk, bufs, io):
    E = w.E
    b = bufs
    ts = bass.ts

    ins = E.dma_start(out=b["identb"][:], in_=io["identb"]) if w.on("sync") else None
    w.inc("sync", ins, "cst", 16)
    if w.on("vector"):
        E.memset(b["ones128"][:], 1.0)
        E.memset(b["ones1"][:], 1.0)
        E.memset(b["epscol"][:, 0:1], 1e-5)
        ins = E.memset(b["epscol"][:, 1:2], 1e-8)
    else:
        ins = None
    w.inc("vector", ins, "lnd", 1)

    mats = [("w_up", io["w_up"], 32, 1024, b["wupT"]),
            ("w_down", io["w_down"], 8, 4096, b["wdT"])]
    for mi, (nm, src_ap, ntiles, tcols, dstT) in enumerate(mats):
        for i in range(ntiles):
            nq = tcols // 1024
            if i >= 1:
                w.wait("sync", "vab", w.cnt["vab"])
            w.wait("sync", "tsg", w.cnt["tsg"])
            ins = E.dma_start(out=b["wprep"][:, 0:tcols],
                              in_=src_ap[i*128:(i+1)*128, :]) if w.on("sync") else None
            w.inc("sync", ins, "wld0", 16)
            w.wait("scalar", "wld0", w.cnt["wld0"])
            for q in range(nq):
                acc = b["acol"][:, 2+(i*nq+q) % 2: 3+(i*nq+q) % 2]
                w.wait("scalar", "vab", max(0, w.cnt["vab"] - 1))
                w.wait("scalar", "sself", w.cnt["sself"])
                ins = E.activation(b["absb"][:], b["wprep"][:, q*1024:(q+1)*1024],
                                   AF.Abs, accum_out=acc) if w.on("scalar") else None
                w.inc("scalar", ins, "sself", 1)
                w.wait("vector", "sself", w.cnt["sself"])
                if i == 0 and q == 0:
                    ins = E.tensor_copy(b["acol"][:, mi:mi+1], acc) if w.on("vector") else None
                else:
                    ins = E.tensor_add(b["acol"][:, mi:mi+1], b["acol"][:, mi:mi+1],
                                       acc) if w.on("vector") else None
                w.inc("vector", ins, "vab", 1)
                w.wait("vector", "vab", w.cnt["vab"])
        w.wait("tensor", "vab", w.cnt["vab"])
        if w.cnt["tred"] >= 1:
            w.wait("tensor", "sbc", w.cnt["sbc"])
        ins = E.matmul(b["ps_sm"].ap()[0:1, 0:1], b["acol"][:, mi:mi+1],
                       b["ones128"][:], start=True, stop=True) if w.on("tensor") else None
        w.inc("tensor", ins, "tred", 1)
        w.wait("scalar", "tred", w.cnt["tred"])
        ins = E.copy(b["alpha1"][:], b["ps_sm"].ap()[0:1, 0:1]) if w.on("scalar") else None
        w.inc("scalar", ins, "sbc", 1)
        w.wait("tensor", "sbc", w.cnt["sbc"])
        ins = E.matmul(b["ps_sm"].ap()[:, 1:2], b["ones1"][:], b["alpha1"][:],
                       start=True, stop=True) if w.on("tensor") else None
        w.inc("tensor", ins, "tred", 1)
        w.wait("scalar", "tred", w.cnt["tred"])
        sc = 0.5 / (128.0 * 32 * 1024)
        if w.on("scalar"):
            E.activation(b["cvec"][:, mi:mi+1], b["ps_sm"].ap()[:, 1:2], AF.Copy,
                         scale=sc)
            ins = E.activation(b["ncvec"][:, mi:mi+1], b["ps_sm"].ap()[:, 1:2], AF.Copy,
                               scale=-sc)
        else:
            ins = None
        w.inc("scalar", ins, "sbc", 1)
        for i in range(ntiles):
            w.wait("sync", "sbc", w.cnt["sbc"])
            if i >= 1:
                w.wait("sync", "tsg", w.cnt["tsg"])
            ins = E.dma_start(out=b["wprep"][:, 0:tcols],
                              in_=src_ap[i*128:(i+1)*128, :]) if w.on("sync") else None
            w.inc("sync", ins, "wld1", 16)
            w.wait("scalar", "wld1", w.cnt["wld1"])
            w.wait("scalar", "vwt", w.cnt["vwt"])
            if w.on("scalar"):
                E.sign(b["sgn1"][:, 0:tcols], b["wprep"][:, 0:tcols],
                       bias=b["ncvec"][:, mi:mi+1])
                ins = E.sign(b["sgn2"][:, 0:tcols], b["wprep"][:, 0:tcols],
                             bias=b["cvec"][:, mi:mi+1])
            else:
                ins = None
            w.inc("scalar", ins, "tsg", 1)
            w.wait("vector", "tsg", w.cnt["tsg"])
            w.wait("vector", "twt", w.cnt["twt"])
            ins = E.tensor_add(b["wt2"][:, 0:tcols], b["sgn1"][:, 0:tcols],
                               b["sgn2"][:, 0:tcols]) if w.on("vector") else None
            w.inc("vector", ins, "vwt", 1)
            nblk = tcols // 128
            w.wait("tensor", "vwt", w.cnt["vwt"])
            w.wait("tensor", "cst", 16)
            for blk in range(0, nblk, 8):
                if w.cnt["twt"] >= 1:
                    w.wait("tensor", "vwc", w.cnt["vwc"])
                for j in range(8):
                    ins = E.transpose(b["ps_xnt"][:, ts(j, 128)],
                                      b["wt2"][:, (blk+j)*128:(blk+j+1)*128],
                                      b["identb"][:]) if w.on("tensor") else None
                    w.inc("tensor", ins, "twt", 1)
                w.wait("vector", "twt", w.cnt["twt"])
                stride = 4096 if mi == 0 else 1024
                if w.on("vector"):
                    ins = None
                    for j in range(8):
                        ins = E.tensor_copy(
                            dstT[:, (blk+j)*stride + i*128: (blk+j)*stride + (i+1)*128],
                            b["ps_xnt"][:, ts(j, 128)])
                else:
                    ins = None
                w.inc("vector", ins, "vwc", 1)

    prep_twt = w.cnt["twt"]
    prep_vwc = w.cnt["vwc"]
    tile_i = 0
    for bt in range(4):
        for t4 in range(4):
            idx = tile_i; tile_i += 1
            par = idx % 2
            cs = 8 * par
            rb = 16 + 8 * par
            if idx >= 1:
                w.wait("sync", "sqd", w.notes[("dsq", idx - 1)])
                w.wait("sync", "lnd", w.notes[("dxn", idx - 1)])
            xb = b["xt"][par]
            ins = E.dma_start(out=xb[:],
                              in_=io["x1"][idx*128:(idx+1)*128, :]) if w.on("sync") else None
            w.inc("sync", ins, "xld%d" % par, 16)
            w.wait("scalar", "xld%d" % par, w.cnt["xld%d" % par])
            ssqa = b["lncol"][:, cs+1:cs+2]
            ssqb = b["lncol"][:, cs+2:cs+3]
            ins = E.activation(b["e1"][:], xb[:, 0:512], AF.Square,
                               accum_out=ssqa) if w.on("scalar") else None
            w.inc("scalar", ins, "sself", 1)
            w.wait("scalar", "sself", w.cnt["sself"])
            ins = E.activation(b["e1"][:], xb[:, 512:1024], AF.Square,
                               accum_out=ssqb) if w.on("scalar") else None
            w.inc("scalar", ins, "sqd", 1)
            w.notes[("dsq", idx)] = w.cnt["sqd"]
            w.wait("vector", "xld%d" % par, w.cnt["xld%d" % par])
            ssum = b["lncol"][:, cs:cs+1]
            mean = b["lncol"][:, cs+3:cs+4]
            m2 = b["lncol"][:, cs+4:cs+5]
            var = b["lncol"][:, cs+5:cs+6]
            rstd = b["lncol"][:, cs+6:cs+7]
            ssqt = b["lncol"][:, cs+7:cs+8]
            ins = E.reduce_sum(ssum, xb[:], axis=AX.X) if w.on("vector") else None
            w.inc("vector", ins, "xtd", 1)
            w.wait("scalar", "xtd", w.cnt["xtd"])
            if idx == 0:
                w.wait("scalar", "lnd", 1)
            ins = E.activation(mean, ssum, AF.Copy, scale=1.0 / D) if w.on("scalar") else None
            w.inc("scalar", ins, "sself", 1)
            w.wait("scalar", "sself", w.cnt["sself"])
            ins = E.activation(m2, mean, AF.Square) if w.on("scalar") else None
            w.inc("scalar", ins, "sqd", 1)
            w.wait("vector", "sqd", w.cnt["sqd"])
            ins = E.reduce_sum(ssqt, b["lncol"][:, cs+1:cs+3],
                               axis=AX.X) if w.on("vector") else None
            w.inc("vector", ins, "vself", 1)
            w.wait("vector", "vself", w.cnt["vself"])
            ins = E.scalar_tensor_tensor(var, ssqt, 1.0 / D, m2, ALU.mult,
                                         ALU.subtract) if w.on("vector") else None
            w.inc("vector", ins, "xtd", 1)
            w.wait("scalar", "xtd", w.cnt["xtd"])
            ins = E.activation(ssqt, var, AF.Ln,
                               bias=b["epscol"][:, 0:1]) if w.on("scalar") else None
            w.inc("scalar", ins, "sself", 1)
            w.wait("scalar", "sself", w.cnt["sself"])
            ins = E.activation(rstd, ssqt, AF.Exp, scale=-0.5) if w.on("scalar") else None
            w.inc("scalar", ins, "sqd", 1)
            w.wait("vector", "sqd", w.cnt["sqd"])
            ins = E.tensor_scalar(b["xn32"][:], xb[:], mean, rstd,
                                  ALU.subtract, ALU.mult) if w.on("vector") else None
            w.inc("vector", ins, "lnd", 1)
            w.notes[("dxn", idx)] = w.cnt["lnd"]
            w.wait("scalar", "lnd", w.cnt["lnd"])
            rsa = b["lncol"][:, rb:rb+1]
            rsb = b["lncol"][:, rb+1:rb+2]
            rsum = b["lncol"][:, rb+2:rb+3]
            lnr = b["lncol"][:, rb+3:rb+4]
            rmsv = b["lncol"][:, rb+4:rb+5]
            tcol = b["lncol"][:, rb+5:rb+6]
            rfold = b["lncol"][:, rb+6:rb+7]
            ins = E.activation(b["e1"][:], b["xn32"][:, 0:512], AF.Square,
                               accum_out=rsa) if w.on("scalar") else None
            w.inc("scalar", ins, "sself", 1)
            w.wait("scalar", "sself", w.cnt["sself"])
            ins = E.activation(b["e1"][:], b["xn32"][:, 512:1024], AF.Square,
                               accum_out=rsb) if w.on("scalar") else None
            w.inc("scalar", ins, "sqd", 1)
            w.wait("vector", "sqd", w.cnt["sqd"])
            ins = E.reduce_sum(rsum, b["lncol"][:, rb:rb+2],
                               axis=AX.X) if w.on("vector") else None
            w.inc("vector", ins, "xtd", 1)
            w.wait("scalar", "xtd", w.cnt["xtd"])
            ins = E.activation(lnr, rsum, AF.Ln, scale=1.0/D) if w.on("scalar") else None
            w.inc("scalar", ins, "sself", 1)
            w.wait("scalar", "sself", w.cnt["sself"])
            ins = E.activation(rmsv, lnr, AF.Exp, scale=0.5) if w.on("scalar") else None
            w.inc("scalar", ins, "sqd", 1)
            w.wait("scalar", "sqd", w.cnt["sqd"])
            ins = E.activation(tcol, rmsv, AF.Copy, scale=2.0,
                               bias=2e-8) if w.on("scalar") else None
            w.inc("scalar", ins, "sqd", 1)
            w.wait("vector", "sqd", w.cnt["sqd"])
            w.wait("vector", "lnd", w.notes.get(("dnorm", idx - 2), 0))
            ins = E.reciprocal(rfold, tcol) if w.on("vector") else None
            w.inc("vector", ins, "grc", 1)
            dst_bf = b["hfm"][:, t4*1024:(t4+1)*1024]
            w.wait("scalar", "grc", w.cnt["grc"])
            w.wait("scalar", "tdn", w.cnt["tdn"])
            ins = E.activation(dst_bf, b["xn32"][:], AF.Copy,
                               scale=rfold) if w.on("scalar") else None
            w.inc("scalar", ins, "lnd", 1)
            w.notes[("dnorm", idx)] = w.cnt["lnd"]
            w.wait("tensor", "lnd", w.cnt["lnd"])
            if tile_i == 1:
                w.wait("tensor", "vwc", prep_vwc)
            w.wait("tensor", "xcd", w.cnt["xcd"])
            w.wait("tensor", "vhf", w.cnt["vhf"])
            for cq in range(8):
                ins = E.transpose(b["ps_xnt"][:, ts(cq, 128)], dst_bf[:, ts(cq, 128)],
                                  b["identb"][:]) if w.on("tensor") else None
                w.inc("tensor", ins, "xtd", 1)
            w.notes[("xnT", idx)] = w.cnt["xtd"]
            w.wait("vector", "xtd", w.cnt["xtd"])
            w.wait("vector", "tup", w.cnt["tup"])
            if w.on("vector"):
                dst = b["xfm4"].ap().rearrange("p (k f) -> p k f", k=8)[:, :, t4*128:(t4+1)*128]
                srcp = b["ps_xnt"].ap().rearrange("p (k f) -> p k f", k=8)
                ins = E.tensor_copy(dst, srcp)
            else:
                ins = None
            w.inc("vector", ins, "xcd", 1)

        for t4 in range(4):
            idx = bt * 4 + t4
            for fs in range(8):
                par = (t4 * 8 + fs) % 2
                w.wait("tensor", "xcd", w.cnt["xcd"])
                if w.cnt["tup"] >= 2:
                    w.wait("tensor", "sge", w.cnt["sge"] - 1)
                for kc in range(8):
                    ins = E.matmul(b["ps_pj"][par][:],
                                   b["xfm4"][:, kc*512 + t4*128: kc*512 + (t4+1)*128],
                                   b["wupT"][:, kc*4096 + fs*512: kc*4096 + (fs+1)*512],
                                   start=(kc == 0), stop=(kc == 7)) if w.on("tensor") else None
                w.inc("tensor", ins, "tup", 1)
                w.wait("scalar", "tup", w.cnt["tup"])
                hs = b["h4"][:, t4*4096 + fs*512: t4*4096 + (fs+1)*512]
                ins = E.activation(hs, b["ps_pj"][par][:], AF.Gelu) if w.on("scalar") else None
                w.inc("scalar", ins, "sself", 1)
                w.wait("scalar", "sself", w.cnt["sself"])
                hb = 32 + 8 * (t4 % 2)
                hsq = b["lncol"][:, hb + fs: hb + fs + 1]
                if fs == 0:
                    w.wait("scalar", "vsq", w.notes.get(("hsqr", idx - 2), 0))
                ins = E.activation(b["e1"][:], hs, AF.Square,
                                   accum_out=hsq) if w.on("scalar") else None
                w.inc("scalar", ins, "sge", 1)
            w.wait("vector", "sge", w.cnt["sge"])
            hb = 32 + 8 * (t4 % 2)
            tot = b["lncol"][:, 48 + t4 % 2: 49 + t4 % 2]
            lnr2 = b["lncol"][:, 50 + t4 % 2: 51 + t4 % 2]
            rms2 = b["lncol"][:, 52 + t4 % 2: 53 + t4 % 2]
            t2c = b["lncol"][:, 54 + t4 % 2: 55 + t4 % 2]
            sc2 = b["lncol"][:, 56 + t4: 57 + t4]
            ins = E.reduce_sum(tot, b["lncol"][:, hb:hb+8],
                               axis=AX.X) if w.on("vector") else None
            w.inc("vector", ins, "vsq", 1)
            w.notes[("hsqr", idx)] = w.cnt["vsq"]
            w.wait("scalar", "vsq", w.cnt["vsq"])
            ins = E.activation(lnr2, tot, AF.Ln, scale=1.0/4096.0) if w.on("scalar") else None
            w.inc("scalar", ins, "sself", 1)
            w.wait("scalar", "sself", w.cnt["sself"])
            ins = E.activation(rms2, lnr2, AF.Exp, scale=0.5) if w.on("scalar") else None
            w.inc("scalar", ins, "sqd", 1)
            w.wait("scalar", "sqd", w.cnt["sqd"])
            w.wait("scalar", "grc", w.notes.get(("sc2", idx - 2), 0))
            ins = E.activation(t2c, rms2, AF.Copy, scale=2.0,
                               bias=2e-8) if w.on("scalar") else None
            w.inc("scalar", ins, "sqd", 1)
            w.wait("vector", "sqd", w.cnt["sqd"])
            w.wait("vector", "sdn", w.notes.get(("sdnend", idx - 4), 0))
            ins = E.reciprocal(sc2, t2c) if w.on("vector") else None
            w.inc("vector", ins, "grc", 1)
            w.notes[("sc2", idx)] = w.cnt["grc"]

        for t4 in range(4):
            idx = bt * 4 + t4
            w.wait("tensor", "sge", w.cnt["sge"])
            for grp in range(4):
                w.wait("tensor", "vhf", w.cnt["vhf"])
                w.wait("tensor", "lnd", w.cnt["lnd"])
                for j in range(8):
                    hc = grp * 8 + j
                    ins = E.transpose(b["ps_xnt"][:, ts(j, 128)],
                                      b["h4"][:, t4*4096 + hc*128: t4*4096 + (hc+1)*128],
                                      b["identb"][:]) if w.on("tensor") else None
                    w.inc("tensor", ins, "thf", 1)
                w.wait("vector", "thf", w.cnt["thf"])
                if grp == 0 and w.cnt["tdn"] >= 1:
                    w.wait("vector", "tdn", w.cnt["tdn"])
                ins = E.tensor_copy(b["hfm"][:, grp*1024:(grp+1)*1024],
                                    b["ps_xnt"][:]) if w.on("vector") else None
                w.inc("vector", ins, "vhf", 1)
            for dh in range(2):
                w.wait("tensor", "vhf", w.cnt["vhf"])
                if w.cnt["tdn"] >= 2:
                    w.wait("tensor", "sdn", w.cnt["sdn"] - 1)
                for hc in range(32):
                    ins = E.matmul(b["ps_pj"][dh][:],
                                   b["hfm"][:, hc*128:(hc+1)*128],
                                   b["wdT"][:, hc*1024 + dh*512: hc*1024 + (dh+1)*512],
                                   start=(hc == 0), stop=(hc == 31)) if w.on("tensor") else None
                w.inc("tensor", ins, "tdn", 1)
                w.wait("scalar", "tdn", w.cnt["tdn"])
                w.wait("scalar", "grc", w.notes[("sc2", idx)])
                w.wait("scalar", "vfin", w.cnt["vfin"])
                w.wait("scalar", "st0", 16 * idx)
                sc2 = b["lncol"][:, 56 + t4: 57 + t4]
                ins = E.activation(b["gsil"][:, dh*512:(dh+1)*512], b["ps_pj"][dh][:],
                                   AF.Silu, scale=sc2) if w.on("scalar") else None
                w.inc("scalar", ins, "sdn", 1)
            w.notes[("sdnend", idx)] = w.cnt["sdn"]
            cp = idx % 2
            if idx >= 2:
                w.wait("sync", "vfin", w.notes.get(("fin", idx - 2), 0))
            ins = E.dma_start(out=b["s1t"][cp][:],
                              in_=io["s1"][idx*128:(idx+1)*128, :]) if w.on("sync") else None
            w.inc("sync", ins, "sld0", 16)
            ins = E.dma_start(out=b["x1t"][cp][:],
                              in_=io["x1"][idx*128:(idx+1)*128, :]) if w.on("sync") else None
            w.inc("sync", ins, "yld0", 16)
            w.wait("vector", "sdn", w.cnt["sdn"])
            w.wait("vector", "sld0", w.cnt["sld0"])
            w.wait("vector", "yld0", w.cnt["yld0"])
            if idx >= 1:
                w.wait("vector", "st0", 16 * idx)
            ins = E.tensor_tensor(b["gsil"][:], b["gsil"][:], b["s1t"][cp][:],
                                  ALU.mult) if w.on("vector") else None
            w.inc("vector", ins, "vself", 1)
            w.wait("vector", "vself", w.cnt["vself"])
            ins = E.tensor_tensor(b["gsil"][:], b["gsil"][:], b["x1t"][cp][:],
                                  ALU.add) if w.on("vector") else None
            w.inc("vector", ins, "vfin", 1)
            w.notes[("fin", idx)] = w.cnt["vfin"]
            w.wait("sync", "vfin", w.cnt["vfin"])
            ins = E.dma_start(out=io["x2"][idx*128:(idx+1)*128, :],
                              in_=b["gsil"][:]) if w.on("sync") else None
            w.inc("sync", ins, "st0", 16)
    if w.on("sync"):
        E.wait_ge(w.sems["st0"], w.cnt["st0"])


def build_phase_d_nc():
    import contextlib
    nc = bass.Bass("TRN2", target_bir_lowering=False, debug=False)
    io = {
        "x1": nc.dram_tensor("x1", [NT, D], F32, kind="ExternalInput").ap(),
        "s1": nc.dram_tensor("s1", [NT, D], F32, kind="ExternalInput").ap(),
        "w_up": nc.dram_tensor("w_up", [4096, D], F32, kind="ExternalInput").ap(),
        "w_down": nc.dram_tensor("w_down", [D, 4096], F32, kind="ExternalInput").ap(),
        "identb": nc.dram_tensor("identb", [128, 128], BF16, kind="ExternalInput").ap(),
        "x2": nc.dram_tensor("x2", [NT, D], F32, kind="ExternalOutput").ap(),
    }
    with contextlib.ExitStack() as ctx:
        bufs = phase_d_bufs(nc, ctx)
        sems = {s: ctx.enter_context(nc.semaphore("sD_" + s)) for s in PHASE_D_SEMS}
        with nc.Block() as blk:
            @blk.sync
            def _(E):
                phase_d_plan(Walk("sync", E, sems), bufs, io)
            @blk.gpsimd
            def _(E):
                phase_d_plan(Walk("gpsimd", E, sems), bufs, io)
            @blk.tensor
            def _(E):
                phase_d_plan(Walk("tensor", E, sems), bufs, io)
            @blk.scalar
            def _(E):
                phase_d_plan(Walk("scalar", E, sems), bufs, io)
            @blk.vector
            def _(E):
                phase_d_plan(Walk("vector", E, sems), bufs, io)
    return nc



def build_fused_nc():
    nc = bass.Bass("TRN2", target_bir_lowering=False, debug=False)
    t = lambda n, s, dt, k="ExternalInput": nc.dram_tensor(n, s, dt, kind=k).ap()
    io = {
        "qx": t("qx", [NT, D], F32),
        "kvx": t("kvx", [T, D], F32),
        "wq": t("wq", [D, D], F32),
        "wk": t("wk", [D, D], F32),
        "wv": t("wv", [D, D], F32),
        "maskf": t("maskf", [128, NUNITS * 8 * HALF], F32),
        "kf": t("kf", [128, 65536], BF16, "Internal"),
        "vf": t("vf", [128, 65536], BF16, "Internal"),
        "identb": t("identb", [128, 128], BF16),
        "identf": t("identf", [128, 128], F32),
        "router_w": t("router_w", [7, D], F32),
        "w_gate": t("w_gate", [7, D, 4096], F32),
        "w_up": t("w_up", [7, D, 4096], F32),
        "w_down": t("w_down", [7, 4096, D], F32),
        "sw_gate": t("sw_gate", [1, D, 4096], F32),
        "sw_up": t("sw_up", [1, D, 4096], F32),
        "sw_down": t("sw_down", [1, 4096, D], F32),
        "wq1": t("wq1", [D, D], F32),
        "wk1": t("wk1", [D, D], F32),
        "wv1": t("wv1", [D, D], F32),
        "wo1": t("wo1", [D, D], F32),
        "selA": t("selA", [64, 16], F32),
        "selB": t("selB", [64, 16], F32),
        "cmask": t("cmask", [128, 128], BF16),
        "l1_w_up": t("l1_w_up", [4096, D], F32),
        "l1_w_down": t("l1_w_down", [D, 4096], F32),
        "s0": t("s0", [NT, D], F32, "Internal"),
        "x1": t("x1", [NT, D], F32, "Internal"),
        "s1": t("s1", [NT, D], F32, "Internal"),
        "cc_in": t("cc_in", [2, 66560], F32, "Internal"),
        "cc_out": t("cc_out", [16, 66560], F32, "Internal"),
        "x2": t("x2", [NT, D], F32, "ExternalOutput"),
    }

    phases = [
        (phase_a_bufs, phase_a_plan, PHASE_A_SEMS, "A",
         dict(qx=io["qx"], kvx=io["kvx"], wq=io["wq"], wk=io["wk"], wv=io["wv"],
              maskf=io["maskf"], kf=io["kf"], vf=io["vf"],
              identb=io["identb"], s0=io["s0"])),
        (phase_b_bufs, phase_b_plan, PHASE_B_SEMS, "B",
         dict(qx=io["qx"], s0=io["s0"], router_w=io["router_w"],
              w_gate=io["w_gate"], w_up=io["w_up"], w_down=io["w_down"],
              sw_gate=io["sw_gate"], sw_up=io["sw_up"], sw_down=io["sw_down"],
              identb=io["identb"], identf=io["identf"], x1=io["x1"])),
        (phase_c_bufs, phase_c_plan, PHASE_C_SEMS, "C",
         dict(x1=io["x1"], wq1=io["wq1"], wk1=io["wk1"], wv1=io["wv1"],
              wo1=io["wo1"], identb=io["identb"], selA=io["selA"],
              selB=io["selB"], cmask=io["cmask"], cc_in=io["cc_in"],
              cc_out=io["cc_out"], s1=io["s1"])),
        (phase_d_bufs, phase_d_plan, PHASE_D_SEMS, "D",
         dict(x1=io["x1"], s1=io["s1"], w_up=io["l1_w_up"],
              w_down=io["l1_w_down"], identb=io["identb"], x2=io["x2"])),
    ]
    for bufs_fn, plan_fn, semnames, tag, pio in phases:
        with nc.cleanup_on_exit():
            with contextlib.ExitStack() as ctx:
                bufs = bufs_fn(nc, ctx)
                sems = {s: nc.alloc_semaphore(f"s{tag}_" + s) for s in semnames}
                with nc.Block() as blk:
                    @blk.sync
                    def _(E, plan_fn=plan_fn, sems=sems, bufs=bufs, pio=pio):
                        plan_fn(Walk("sync", E, sems), bufs, pio)
                    @blk.gpsimd
                    def _(E, plan_fn=plan_fn, sems=sems, bufs=bufs, pio=pio):
                        plan_fn(Walk("gpsimd", E, sems), bufs, pio)
                    @blk.tensor
                    def _(E, plan_fn=plan_fn, sems=sems, bufs=bufs, pio=pio):
                        plan_fn(Walk("tensor", E, sems), bufs, pio)
                    @blk.scalar
                    def _(E, plan_fn=plan_fn, sems=sems, bufs=bufs, pio=pio):
                        plan_fn(Walk("scalar", E, sems), bufs, pio)
                    @blk.vector
                    def _(E, plan_fn=plan_fn, sems=sems, bufs=bufs, pio=pio):
                        plan_fn(Walk("vector", E, sems), bufs, pio)
                nc.all_engine_barrier()
    return nc


def host_fused_inputs(c, x, weights):
    m = host_phase_a_inputs(c, x)
    m.update(host_phase_c_inputs(c))
    m["identf"] = np.eye(128, dtype=np.float32)
    m.update(weights)
    return m


from concourse.bass_utils import run_bass_kernel_spmd

_NC_CACHE = {}

def _get_nc(name, builder):
    if name not in _NC_CACHE:
        _NC_CACHE[name] = builder()
    return _NC_CACHE[name]


def kernel(L0_attn_x, L0_attn_Wk, L0_attn_Wq, L0_attn_Wv, L0_ffn_router_w,
           L0_ffn_w_gate, L0_ffn_w_up, L0_ffn_w_down, L0_ffn_sw_gate,
           L0_ffn_sw_up, L0_ffn_sw_down, L1_attn_Wq, L1_attn_Wk, L1_attn_Wv,
           L1_attn_Wo, L1_ffn_w_up, L1_ffn_w_down):
    x = np.ascontiguousarray(np.asarray(L0_attn_x, dtype=np.float32))
    cores = list(range(8))
    weights = {
        "wq": L0_attn_Wq, "wk": L0_attn_Wk, "wv": L0_attn_Wv,
        "router_w": L0_ffn_router_w, "w_gate": L0_ffn_w_gate,
        "w_up": L0_ffn_w_up, "w_down": L0_ffn_w_down,
        "sw_gate": L0_ffn_sw_gate, "sw_up": L0_ffn_sw_up,
        "sw_down": L0_ffn_sw_down,
        "wq1": L1_attn_Wq, "wk1": L1_attn_Wk, "wv1": L1_attn_Wv,
        "wo1": L1_attn_Wo, "l1_w_up": L1_ffn_w_up, "l1_w_down": L1_ffn_w_down,
    }
    nc = _get_nc("fused", build_fused_nc)
    in_maps = [host_fused_inputs(c, x, weights) for c in cores]
    res = run_bass_kernel_spmd(nc, in_maps, core_ids=cores)
    out = np.zeros((2, 8192, 1024), dtype=np.float32)
    for c in cores:
        host_scatter_tokens(c, out, res.results[c]["x2"])
    return out
